# version 27
# baseline (speedup 1.0000x reference)
import time
import numpy as np
import ml_dtypes
from contextlib import ExitStack

import concourse.bass as bass
import concourse.mybir as mybir
from concourse.bass_utils import run_bass_kernel_spmd

F32 = mybir.dt.float32
BF16 = mybir.dt.bfloat16
I32 = mybir.dt.int32
AF = mybir.ActivationFunctionType
ALU = mybir.AluOpType
AX = mybir.AxisListType

D = 2048
TPC = 4096
TT = 1024
NT = TPC // TT
NB = 16
NK = 16
EPS = 1e-5
LACLAMP = -2e-6
INV_D = 1.0 / D

_MODE = "full"


def build_nc(mode):
    nc = bass.Bass("TRN2", target_bir_lowering=False)

    x_in = nc.dram_tensor("x", [D, TPC], F32, kind="ExternalInput").ap()
    wb_in = nc.dram_tensor("wb", [NB, D, 128], F32, kind="ExternalInput").ap()
    wa_in = nc.dram_tensor("wa", [NB, D, 128], F32, kind="ExternalInput").ap()
    wx_in = nc.dram_tensor("wx", [NB, D, 128], F32, kind="ExternalInput").ap()
    rt_in = nc.dram_tensor("rt", [D, 8], F32, kind="ExternalInput").ap()
    at_in = nc.dram_tensor("at", [D, 128], F32, kind="ExternalInput").ap()
    bwt_in = nc.dram_tensor("bwt", [128, D], F32, kind="ExternalInput").ap()
    bvec_in = nc.dram_tensor("bvec", [128, 4 * NB], F32, kind="ExternalInput").ap()
    hmask_in = nc.dram_tensor("hmask", [128, 1], F32, kind="ExternalInput").ap()
    slot_in = nc.dram_tensor("slot", [1, 1], I32, kind="ExternalInput").ap()
    nonce_in = nc.dram_tensor("nonce", [1, 8], I32, kind="ExternalInput").ap()

    out_d = nc.dram_tensor("out", [NT, NB, 128, TT], F32, kind="ExternalOutput").ap()
    dbg_d = None
    if mode == "x2p":
        dbg_d = nc.dram_tensor("dbg", [NT, 8, 128, TT], F32, kind="ExternalOutput").ap()
    dbgf_d = None
    if mode == "ffn1":
        dbgf_d = nc.dram_tensor("dbgf", [20, 2048], F32, kind="ExternalOutput").ap()

    wbb = nc.dram_tensor("wbb", [NB, 128, D], BF16, kind="Internal")
    wab = nc.dram_tensor("wab", [NB, 128, D], BF16, kind="Internal")
    wxb = nc.dram_tensor("wxb", [NB, 128, D], BF16, kind="Internal")
    x2p_d = nc.dram_tensor("x2pd", [NT, NB, 128, TT], F32, kind="Internal")
    p_d = nc.dram_tensor("pdd", [NT, NB, 128, TT], BF16, kind="Internal")
    ex_d = nc.dram_tensor("exd", [2, 128, NB], F32, kind="Internal", addr_space="Shared")
    fl_d = nc.dram_tensor("fld", [2, 1, 8], I32, kind="Internal", addr_space="Shared")

    ctx = ExitStack()
    with ctx:
        def sbt(name, shape, dt=F32):
            return ctx.enter_context(nc.sbuf_tensor(name, shape, dt))

        def sem(name):
            return ctx.enter_context(nc.semaphore(name))

        x1b = sbt("x1b", [128, NB * TT])
        xbf = sbt("xbf", [128, NB * TT], BF16)
        wpool = sbt("wpool", [128, 6 * D], BF16)
        sbAT = sbt("sbAT", [128, D], BF16)
        sbBwT = sbt("sbBwT", [128, D], BF16)
        sbRt = sbt("sbRt", [128, NK * 8], BF16)
        S = [sbt(f"gs{i}", [128, TT]) for i in range(6)]
        tmp_e = sbt("tmp_e", [128, 512])
        bacc = sbt("bacc", [128, TT], BF16)
        bacc2 = sbt("bacc2", [128, TT], BF16)
        bsq = sbt("bsq", [128, TT], BF16)
        mB = sbt("mB", [128, TT])
        rsB = sbt("rsB", [128, TT])
        rsBh = sbt("rsBh", [128, TT])
        rows = sbt("rows", [1, 3 * TT])
        mbf = sbt("mbf", [1, TT], BF16)
        maskbf = sbt("maskbf", [128, TT], BF16)
        hbf = sbt("hbf", [128, TT], BF16)
        oh_sb = sbt("oh_sb", [128, 64], BF16)
        ohT = sbt("ohT", [8, TT], BF16)
        pstage = sbt("pstage", [128, 2 * TT], BF16)
        ysave = sbt("ysave", [128, NB])
        csave = sbt("csave", [128, NB])
        bload = sbt("bload", [128, 4 * NB])
        k1h = sbt("k1h", [128, NB])
        negba = sbt("negba", [128, NB])
        negbx = sbt("negbx", [128, NB])
        hbrow = sbt("hbrow", [128, NB])
        ksc = sbt("ksc", [128, 2 * NB])
        wsumn = sbt("wsumn", [1, D], BF16)
        asumn = sbt("asumn", [1, 128], BF16)
        rsumn = sbt("rsumn", [1, 8], BF16)
        onesb = sbt("onesb", [128, 1], BF16)
        onesf = sbt("onesf", [1, 128])
        id_i = sbt("id_i", [128, 128], I32)
        idbf = sbt("idbf", [128, 128], BF16)
        ex_i = sbt("ex_i", [8, 128], I32)
        exbf = sbt("exbf", [8, 128], BF16)
        hraw = sbt("hraw", [128, NB])
        hfin = sbt("hfin", [128, NB])
        hmk = sbt("hmk", [128, 1])
        sbslot = sbt("sbslot", [1, 1], I32)
        sbnon = sbt("sbnon", [1, 8], I32)
        sbfl = sbt("sbfl", [1, 16], I32)
        epsrow = sbt("epsrow", [1, 1])
        rmaxc = sbt("rmaxc", [128, 8])
        topguard = sbt("topguard", [128, 1024])

        pool = [ctx.enter_context(nc.psum_tensor(f"pp{i}", [128, 512], F32)) for i in range(6)]
        ps_row = ctx.enter_context(nc.psum_tensor("ps_row", [1, 512], F32))
        ps_tp = ctx.enter_context(nc.psum_tensor("ps_tp", [8, TT], BF16))

        s_ld = sem("s_ld");  s_cast = sem("s_cast");  s_giota = sem("s_giota")
        s_vinit = sem("s_vinit");  s_sinit = sem("s_sinit")
        s_tsetup = sem("s_tsetup");  s_vsetup = sem("s_vsetup")
        s_x0 = sem("s_x0");  s_xbf = sem("s_xbf")
        s_wb = sem("s_wb");  s_wg = sem("s_wg")
        s_vst = sem("s_vst");  s_trow = sem("s_trow");  s_vrow = sem("s_vrow")
        s_vfin = sem("s_vfin");  s_sfin = sem("s_sfin");  s_vrsh = sem("s_vrsh")
        s_tbc1 = sem("s_tbc1");  s_sbc1 = sem("s_sbc1")
        s_tbc2 = sem("s_tbc2");  s_sbc2 = sem("s_sbc2")
        s_trt = sem("s_trt");  s_voh = sem("s_voh");  s_ttp = sem("s_ttp")
        s_soh = sem("s_soh");  s_tmask = sem("s_tmask");  s_smask = sem("s_smask")
        s_tA = sem("s_tA");  s_vhbf = sem("s_vhbf")
        s_tbase = sem("s_tbase");  s_vevac = sem("s_vevac")
        s_vap = sem("s_vap");  s_tgate = sem("s_tgate");  s_sevac = sem("s_sevac")
        s_sr = sem("s_sr");  s_si = sem("s_si");  s_vla = sem("s_vla")
        s_sa = sem("s_sa");  s_vy = sem("s_vy");  s_sp9 = sem("s_sp9")
        s_gx = sem("s_gx");  s_spill = sem("s_spill");  s_pspill = sem("s_pspill")
        s_pld = sem("s_pld");  s_exch = sem("s_exch");  s_poll = sem("s_poll")
        s_hfin = sem("s_hfin");  s_out = sem("s_out")
        s_wsum = sem("s_wsum");  s_vwsum = sem("s_vwsum");  s_ysv = sem("s_ysv")
        s_vrx = sem("s_vrx");  s_srx = sem("s_srx");  s_wsld = sem("s_wsld")
        s_smbf = sem("s_smbf");  s_ydbg = sem("s_ydbg")
        s_cwb = sem("s_cwb");  s_cmisc = sem("s_cmisc")
        s_cwa = sem("s_cwa");  s_cwx = sem("s_cwx")
        s_wsldP = [sem("s_wsldA"), sem("s_wsldB")]
        s_wbP = [sem("s_wbA"), sem("s_wbB")]
        s_wgaP = [sem("s_wgaA"), sem("s_wgaB")]
        s_wgxP = [sem("s_wgxA"), sem("s_wgxB")]
        s_pldP = [sem("s_pldA"), sem("s_pldB")]
        s_x2rP = [sem("s_x2rA"), sem("s_x2rB")]

        block = ctx.enter_context(nc.Block())

        if mode == "ffn1":
            ffns = [(f, "out") for f in range(NT)]
            gate_tiles = []
        elif mode == "x2p":
            ffns = [(f, "x1") for f in range(NT)]
            gate_tiles = list(range(NT))
        else:
            ffns = [(f, "x1") for f in range(NT)] + [(f, "out") for f in range(NT, 2 * NT)]
            gate_tiles = list(range(NT))

        def inst_of(f):
            if mode == "ffn1":
                return f
            if f < NT:
                return 2 * f
            return 2 * NT + (f - NT)

        x2p_spill = out_d if mode == "x2p" else x2p_d.ap()

        @block.sync
        def _(y):
            y.dma_start(out=bload[:], in_=bvec_in[:]).then_inc(s_ld, 16)
            y.dma_start(out=hmk[:], in_=hmask_in[:]).then_inc(s_ld, 16)
            y.dma_start(out=sbslot[:], in_=slot_in[:]).then_inc(s_ld, 16)
            y.dma_start(out=sbnon[:], in_=nonce_in[:]).then_inc(s_ld, 16)
            if dbgf_d is not None:
                y.wait_ge(s_vsetup, 1)
                y.dma_start(out=dbgf_d[16:17, 0:64],
                            in_=asumn[0:1, :].bitcast(F32)).then_inc(s_ydbg, 16)
                y.wait_ge(s_vsetup, 2)
                y.dma_start(out=dbgf_d[16:17, 64:68],
                            in_=rsumn[0:1, :].bitcast(F32)).then_inc(s_ydbg, 16)
            y.wait_ge(s_cwb, 256)
            for b in range(NB):
                if b >= 2:
                    y.wait_ge(s_wsum, b - 1)
                y.dma_start(out=wpool[:, (b % 2) * D:(b % 2 + 1) * D],
                            in_=wbb.ap()[b]).then_inc(s_wsldP[b % 2], 16)

            for f, dst in ffns:
                t = f if f < NT else f - NT
                if f < NT:
                    if t > 0:
                        if gate_tiles:
                            y.wait_ge(s_spill, 256 * t)
                        else:
                            y.wait_ge(s_vevac, 32 * t)
                    y.dma_start(
                        out=x1b[:].rearrange("p (k t) -> p k t", k=NK),
                        in_=x_in[:, t * TT:(t + 1) * TT].rearrange("(k p) t -> p k t", p=128),
                    ).then_inc(s_x0, 16)
                for b in range(NB):
                    idx = NK * f + b
                    if idx >= 2:
                        y.wait_ge(s_tbase, 2 * (idx - 1))
                    elif idx == 0:
                        y.wait_ge(s_wsum, NB - 1)
                    else:
                        y.wait_ge(s_wsum, NB)
                    y.dma_start(
                        out=wpool[:, (idx % 2) * D:(idx % 2 + 1) * D],
                        in_=wbb.ap()[b],
                    ).then_inc(s_wbP[idx % 2], 16)
                    if dst == "out" and b >= 2:
                        bo = b - 2
                        y.wait_ge(s_vevac, 32 * f + 2 * (bo + 1))
                        y.dma_start(out=out_d[t, bo], in_=S[bo % 2][:, 0:TT]).then_inc(s_out, 16)
                if dst == "out":
                    for bo in (NB - 2, NB - 1):
                        y.wait_ge(s_vevac, 32 * f + 2 * (bo + 1))
                        y.dma_start(out=out_d[t, bo], in_=S[bo % 2][:, 0:TT]).then_inc(s_out, 16)
                if f < NT and t in gate_tiles:
                    for b in range(NB):
                        if b >= 2:
                            y.wait_ge(s_tgate, 32 * t + 2 * (b - 2) + 2)
                        elif t > 0:
                            y.wait_ge(s_tgate, 32 * (t - 1) + 2 * (NB + b - 2) + 2)
                        if t == 0 and b == 0:
                            y.wait_ge(s_cwa, 256)
                            y.wait_ge(s_cwx, 256)
                        y.dma_start(
                            out=wpool[:, (2 + b % 2) * D:(2 + b % 2 + 1) * D],
                            in_=wab.ap()[b],
                        ).then_inc(s_wgaP[b % 2], 16)
                        y.dma_start(
                            out=wpool[:, (4 + b % 2) * D:(4 + b % 2 + 1) * D],
                            in_=wxb.ap()[b],
                        ).then_inc(s_wgxP[b % 2], 16)
            if mode == "x2p":
                y.wait_ge(s_spill, 256 * NT)
                y.wait_ge(s_pspill, 256 * NT)
            else:
                y.wait_ge(s_out, 16 * NB * NT)

        @block.gpsimd
        def _(g):
            g.iota(id_i[:], pattern=[[1, 128]], base=0, channel_multiplier=-1)
            g.iota(ex_i[:], pattern=[[1, 8], [0, 16]], base=0,
                   channel_multiplier=-1).then_inc(s_giota, 1)
            g.dma_start(
                out=xbf[:].rearrange("p (k t) -> p k t", k=NK),
                in_=x_in[:, 0:TT].rearrange("(k p) t -> p k t", p=128),
            ).then_inc(s_xbf, 16)
            for b in range(NB):
                g.dma_start(out=wbb.ap()[b].rearrange("p (k o) -> p k o", k=NK),
                            in_=wb_in[b].rearrange("(k p) o -> p k o", p=128)).then_inc(s_cwb, 16)
            g.dma_start(out=sbAT[:].rearrange("p (k e) -> p k e", k=NK),
                        in_=at_in.rearrange("(k p) e -> p k e", p=128)).then_inc(s_cmisc, 16)
            g.dma_start(out=sbBwT[:], in_=bwt_in[:]).then_inc(s_cmisc, 16)
            g.dma_start(out=sbRt[:].rearrange("p (k e) -> p k e", k=NK),
                        in_=rt_in.rearrange("(k p) e -> p k e", p=128)).then_inc(s_cmisc, 16)
            for b in range(NB):
                g.dma_start(out=wab.ap()[b].rearrange("p (k o) -> p k o", k=NK),
                            in_=wa_in[b].rearrange("(k p) o -> p k o", p=128)).then_inc(s_cwa, 16)
            for b in range(NB):
                g.dma_start(out=wxb.ap()[b].rearrange("p (k o) -> p k o", k=NK),
                            in_=wx_in[b].rearrange("(k p) o -> p k o", p=128)).then_inc(s_cwx, 16)

            if dbgf_d is not None:
                g.wait_ge(s_vwsum, NB)
                g.dma_start(out=dbgf_d[0:1, :], in_=wsumn[0:1, :]).then_inc(s_poll, 16)
                g.wait_ge(s_vsetup, 2)
                g.dma_start(out=dbgf_d[1:2, 0:128], in_=asumn[0:1, :]).then_inc(s_poll, 16)
                g.dma_start(out=dbgf_d[1:2, 128:136], in_=rsumn[0:1, :]).then_inc(s_poll, 16)
                g.wait_ge(s_vinit, 8)
                g.dma_start(out=dbgf_d[2:3, :].rearrange("a (p c) -> (a p) c", p=128),
                            in_=hbrow[:, 0:NB]).then_inc(s_poll, 16)
                g.dma_start(out=dbgf_d[3:4, 0:1024],
                            in_=wsumn[0:1, :].bitcast(F32)).then_inc(s_poll, 16)
                g.dma_start(out=dbgf_d[3:4, 1024:1088],
                            in_=asumn[0:1, :].bitcast(F32)).then_inc(s_poll, 16)
                g.dma_start(out=dbgf_d[3:4, 1088:1092],
                            in_=rsumn[0:1, :].bitcast(F32)).then_inc(s_poll, 16)
            for t in range(NT):
                if t > 0 and not gate_tiles:
                    g.wait_ge(s_tbase, 32 * t)
                    g.dma_start(
                        out=xbf[:].rearrange("p (k t) -> p k t", k=NK),
                        in_=x_in[:, t * TT:(t + 1) * TT].rearrange("(k p) t -> p k t", p=128),
                    ).then_inc(s_xbf, 16)
                if dbgf_d is not None:
                    g.wait_ge(s_sfin, t + 1)
                    g.dma_start(out=dbgf_d[4 + 2 * t:5 + 2 * t, 0:TT],
                                in_=rows[0:1, 0:TT]).then_inc(s_poll, 16)
                    g.dma_start(out=dbgf_d[5 + 2 * t:6 + 2 * t, 0:TT],
                                in_=rows[0:1, TT:2 * TT]).then_inc(s_poll, 16)
                if t not in gate_tiles:
                    continue
                for b in range(NB):
                    g.wait_ge(s_vy, 2 * (NB * t + b) + 1)
                    if dbg_d is not None and b == 0:
                        for di, sl in enumerate((4, 2, 5, 1, 0, 3)):
                            g.dma_start(out=dbg_d[t, di], in_=S[sl][:, 0:TT]).then_inc(s_poll, 16)
                        if t > 0:
                            g.dma_start(out=dbg_d[t, 6][:, 0:NB], in_=ysave[:]).then_inc(s_poll, 16)
                            g.dma_start(out=dbg_d[t, 7][:, 0:NB], in_=csave[:]).then_inc(s_poll, 16)
                        g.wait_ge(s_poll, 128 * (t + 1) - 32)
                    g.tensor_tensor(out=x1b[:, b * TT:(b + 1) * TT],
                                    in0=x1b[:, b * TT:(b + 1) * TT],
                                    in1=S[4][:, 0:TT], op=ALU.add).then_inc(s_gx, 1)
                    g.dma_start(out=x2p_spill[t, b],
                                in_=x1b[:, b * TT:(b + 1) * TT]).then_inc(s_spill, 16)
                    g.wait_ge(s_sp9, NB * t + b + 1)
                    g.dma_start(out=p_d.ap()[t, b],
                                in_=pstage[:, (b % 2) * TT:(b % 2 + 1) * TT]).then_inc(s_pspill, 16)
                    if b == 13 and t + 1 < NT:
                        g.wait_ge(s_tgate, 32 * (t + 1))
                        g.dma_start(
                            out=xbf[:].rearrange("p (k t) -> p k t", k=NK),
                            in_=x_in[:, (t + 1) * TT:(t + 2) * TT].rearrange(
                                "(k p) t -> p k t", p=128),
                        ).then_inc(s_xbf, 16)
            if mode != "full":
                if dbgf_d is not None:
                    g.wait_ge(s_out, 16 * NB * NT)
                    g.dma_start(out=dbgf_d[12:13, 0:128],
                                in_=asumn[0:1, :]).then_inc(s_poll, 16)
                    g.dma_start(out=dbgf_d[12:13, 128:136],
                                in_=rsumn[0:1, :]).then_inc(s_poll, 16)
                    g.dma_start(out=dbgf_d[13:14, 0:1024],
                                in_=wsumn[0:1, :].bitcast(F32)).then_inc(s_poll, 16)
                    g.dma_start(out=dbgf_d[13:14, 1024:1088],
                                in_=asumn[0:1, :].bitcast(F32)).then_inc(s_poll, 16)
                    g.dma_start(out=dbgf_d[13:14, 1088:1092],
                                in_=rsumn[0:1, :].bitcast(F32)).then_inc(s_poll, 16)
                    g.dma_start(out=dbgf_d[14:15, 0:1024],
                                in_=mbf[0:1, :]).then_inc(s_poll, 16)
                    g.dma_start(out=dbgf_d[15:16, 0:512],
                                in_=mbf[0:1, :].bitcast(F32)).then_inc(s_poll, 16)
                    g.dma_start(out=dbgf_d[17:18, 0:1024],
                                in_=topguard[0:1, :]).then_inc(s_poll, 16)
                    g.dma_start(out=dbgf_d[18:19, 0:1024],
                                in_=topguard[127:128, :]).then_inc(s_poll, 16)
                    g.wait_ge(s_poll, 16 * (7 + 2 * NT + 9))
                    g.wait_ge(s_ydbg, 32)
                return
            g.wait_ge(s_ysv, 2 * NB * NT)
            with (
                g.register("slotr") as slotr,
                g.register("nonr") as nonr,
                g.register("more") as more,
                g.register("cnt") as cnt,
                g.register("w") as w,
            ):
                g.wait_ge(s_ld, 64)
                g.reg_load(slotr, sbslot[0:1, 0:1])
                g.reg_load(nonr, sbnon[0:1, 0:1])
                sidx = g.snap(slotr, min_val=0, max_val=1)
                g.dma_start(out=ex_d.ap()[sidx], in_=ysave[:]).then_inc(s_exch, 16)
                g.wait_ge(s_exch, 16)
                g.dma_start(out=fl_d.ap()[sidx], in_=sbnon[:]).then_inc(s_exch, 16)
                g.wait_ge(s_exch, 32)
                g.reg_mov(more, 1)
                with g.While(more):
                    g.dma_start(out=sbfl[:],
                                in_=fl_d.ap().rearrange("a b c -> b (a c)")).then_inc(s_poll, 16)
                    g.drain()
                    g.reg_mov(cnt, 0)
                    for sl in range(2):
                        for k in range(4):
                            g.reg_load(w, sbfl[0:1, 8 * sl + k:8 * sl + k + 1])
                            with g.If_eq(w, nonr):
                                g.reg_add(cnt, cnt, 1)
                    g.reg_mov(more, 1)
                    with g.If_eq(cnt, 8):
                        g.reg_mov(more, 0)
                g.dma_start(out=hraw[:], in_=ex_d.ap()[0]).then_inc(s_exch, 16)
            g.wait_ge(s_exch, 48)
            g.tensor_scalar(out=hfin[:], in0=hraw[:], scalar1=hmk[:, 0:1],
                            scalar2=None, op0=ALU.mult).then_inc(s_hfin, 1)
            g.wait_ge(s_pspill, 256 * NT)
            for t in range(NT):
                for b in range(NB):
                    if NB * t + b >= 2:
                        g.wait_ge(s_gx, NB * NT + 2 * (NB * t + b - 1))
                    g.dma_start(out=pstage[:, (b % 2) * TT:(b % 2 + 1) * TT],
                                in_=p_d.ap()[t, b]).then_inc(s_pldP[b % 2], 16)

        @block.tensor
        def _(t_):
            def setup_sums():
                t_.wait_ge(s_cwb, 256)
                t_.wait_ge(s_cmisc, 48)
                for k in range(NK):
                    t_.matmul(pool[2][0:1, 0:128], lhsT=onesb[:, 0:1],
                              rhs=sbAT[:, k * 128:(k + 1) * 128],
                              start=(k == 0), stop=(k == NK - 1))
                t_.drain().then_inc(s_tsetup, 1)
                for k in range(NK):
                    t_.matmul(pool[3][0:1, 0:8], lhsT=onesb[:, 0:1],
                              rhs=sbRt[:, k * 8:(k + 1) * 8],
                              start=(k == 0), stop=(k == NK - 1))
                t_.drain().then_inc(s_tsetup, 1)
                for b in range(NB):
                    t_.wait_ge(s_wsldP[b % 2], 16 * (b // 2 + 1))
                    if b >= 2:
                        t_.wait_ge(s_vwsum, b - 1)
                    for k in range(NK):
                        t_.matmul(pool[4 + (b % 2)][0:1, 0:128], lhsT=onesb[:, 0:1],
                                  rhs=wpool[:, (b % 2) * D + k * 128:(b % 2) * D + (k + 1) * 128],
                                  start=(k == 0), stop=(k == NK - 1))
                    t_.drain().then_inc(s_wsum, 1)

            for f, dst in ffns:
                t = f if f < NT else f - NT
                inst = inst_of(f)
                rb = 4 * inst
                if f == 0:
                    t_.wait_ge(s_vinit, 8)
                t_.wait_ge(s_vst, 2 * (inst + 1))
                for j in range(4):
                    h = j % 2
                    t_.wait_ge(s_vrow, rb + j)
                    if j < 2:
                        for k in range(NK):
                            t_.matmul(ps_row[0:1, 0:512], lhsT=onesb[:, 0:1],
                                      rhs=xbf[:, k * TT + h * 512:k * TT + h * 512 + 512],
                                      start=(k == 0), stop=(k == NK - 1))
                    else:
                        t_.matmul(ps_row[0:1, 0:512], lhsT=onesb[:, 0:1],
                                  rhs=bacc2[:, h * 512:(h + 1) * 512],
                                  start=True, stop=True)
                    t_.drain().then_inc(s_trow, 1)
                t_.wait_ge(s_vrsh, f + 1)
                if f > 0:
                    t_.wait_ge(s_vevac, 32 * f)
                    if gate_tiles and f <= NT:
                        t_.wait_ge(s_sevac, 32 * min(f, NT))
                for h in range(2):
                    mm = t_.matmul(pool[h][:, 0:512], lhsT=onesf[0:1, :],
                                   rhs=rows[0:1, 2 * TT + h * 512:2 * TT + (h + 1) * 512],
                                   start=True, stop=True)
                t_.drain().then_inc(s_tbc1, 1)
                if f == 0:
                    setup_sums()
                    t_.wait_ge(s_vsetup, 2)
                    t_.wait_ge(s_vwsum, NB)
                t_.wait_ge(s_smbf, inst + 1)
                for c8 in range(8):
                    for k in range(NK):
                        t_.matmul(pool[4][:, 8 * c8:8 * c8 + 8],
                                  lhsT=xbf[:, k * TT + c8 * 128:k * TT + (c8 + 1) * 128],
                                  rhs=sbRt[:, k * 8:(k + 1) * 8],
                                  start=(k == 0), stop=False)
                    mm = t_.matmul(pool[4][:, 8 * c8:8 * c8 + 8],
                                   lhsT=mbf[0:1, c8 * 128:(c8 + 1) * 128],
                                   rhs=rsumn[0:1, :], start=False, stop=True)
                t_.drain().then_inc(s_trt, 1)
                for h in range(2):
                    for k in range(NK):
                        t_.matmul(pool[2 + h][:, 0:512],
                                  lhsT=sbAT[:, k * 128:(k + 1) * 128],
                                  rhs=xbf[:, k * TT + h * 512:k * TT + (h + 1) * 512],
                                  start=(k == 0), stop=False)
                    t_.matmul(pool[2 + h][:, 0:512], lhsT=asumn[0:1, :],
                              rhs=mbf[0:1, h * 512:(h + 1) * 512],
                              start=False, stop=True)
                    t_.drain().then_inc(s_tA, 1)
                t_.wait_ge(s_voh, f + 1)
                for c8 in range(8):
                    mm = t_.transpose(ps_tp[0:8, c8 * 128:(c8 + 1) * 128],
                                      in_=oh_sb[:, 8 * c8:8 * (c8 + 1)], identity=idbf[:])
                t_.drain().then_inc(s_ttp, 1)
                t_.wait_ge(s_soh, f + 1)
                for h in range(2):
                    mm = t_.matmul(pool[4 + h][:, 0:512], lhsT=exbf[:, :],
                                   rhs=ohT[0:8, h * 512:(h + 1) * 512],
                                   start=True, stop=True)
                t_.drain().then_inc(s_tmask, 1)
                for u in range(32):
                    b, h = u // 2, u % 2
                    bank = u % 6
                    if u >= 6:
                        t_.wait_ge(s_vevac, 32 * f + u - 5)
                    elif bank in (0, 1):
                        t_.wait_ge(s_sbc1, 2 * (f + 1))
                    elif bank in (2, 3):
                        t_.wait_ge(s_vhbf, 2 * (f + 1))
                    else:
                        t_.wait_ge(s_smask, 2 * (f + 1))
                    t_.wait_ge(s_wbP[(NK * f + b) % 2], 16 * ((NK * f + b) // 2 + 1))
                    wslot = (NK * f + b) % 2
                    for k in range(NK):
                        t_.matmul(pool[bank][:, 0:512],
                                  lhsT=wpool[:, wslot * D + k * 128:wslot * D + (k + 1) * 128],
                                  rhs=xbf[:, k * TT + h * 512:k * TT + (h + 1) * 512],
                                  start=(k == 0), stop=False)
                    t_.matmul(pool[bank][:, 0:512],
                              lhsT=wsumn[0:1, b * 128:(b + 1) * 128],
                              rhs=mbf[0:1, h * 512:(h + 1) * 512],
                              start=False, stop=False)
                    t_.matmul(pool[bank][:, 0:512],
                              lhsT=sbBwT[:, b * 128:(b + 1) * 128],
                              rhs=hbf[:, h * 512:(h + 1) * 512],
                              start=False, stop=True)
                    t_.drain().then_inc(s_tbase, 1)
                if f < NT and t in gate_tiles:
                    inst2 = 2 * t + 1
                    rb2 = 4 * inst2
                    t_.wait_ge(s_vst, 2 * (inst2 + 1))
                    for j in range(4):
                        src = bacc if j < 2 else bacc2
                        h = j % 2
                        t_.wait_ge(s_vrow, rb2 + j)
                        t_.matmul(ps_row[0:1, 0:512], lhsT=onesb[:, 0:1],
                                  rhs=src[:, h * 512:(h + 1) * 512],
                                  start=True, stop=True)
                        t_.drain().then_inc(s_trow, 1)
                    t_.wait_ge(s_sfin, inst2 + 1)
                    t_.wait_ge(s_vfin, inst2 + 1)
                    t_.wait_ge(s_vevac, 32 * (f + 1))
                    for j in range(4):
                        srcoff = (0 if j < 2 else TT) + (j % 2) * 512
                        mm = t_.matmul(pool[j][:, 0:512], lhsT=onesf[0:1, :],
                                       rhs=rows[0:1, srcoff:srcoff + 512],
                                       start=True, stop=True)
                    t_.drain().then_inc(s_tbc2, 1)
                    for c in range(32):
                        b, gate = c // 2, c % 2
                        cg = 32 * t + c
                        bank0, bank1 = (2 * c) % 6, (2 * c + 1) % 6
                        if c >= 3:
                            t_.wait_ge(s_sevac, cg - 2)
                        else:
                            t_.wait_ge(s_sbc2, 4 * (t + 1))
                        _ig = NK * t + b
                        t_.wait_ge((s_wgaP if gate == 0 else s_wgxP)[_ig % 2],
                                   16 * (_ig // 2 + 1))
                        woff = ((2 if gate == 0 else 4) + b % 2) * D
                        for h in range(2):
                            t_.wait_ge(s_vap, 32 * t + 16 * (h + 1))
                            for k in range(NK):
                                mm = t_.matmul(pool[(bank0, bank1)[h]][:, 0:512],
                                               lhsT=wpool[:, woff + k * 128:woff + (k + 1) * 128],
                                               rhs=xbf[:, k * TT + h * 512:k * TT + (h + 1) * 512],
                                               start=(k == 0), stop=(k == NK - 1))
                        t_.drain().then_inc(s_tgate, 1)

        @block.scalar
        def _(s):
            s.wait_ge(s_vinit, 8)
            s.activation(out=ksc[:, 0:NB], in_=bload[:, 3 * NB:4 * NB], func=AF.Exp,
                         bias=0.0, scale=-1.0).then_inc(s_sinit, 1)
            s.wait_ge(s_sinit, 1)
            s.activation(out=ksc[:, NB:2 * NB], in_=ksc[:, 0:NB], func=AF.Ln,
                         bias=1.0, scale=1.0).then_inc(s_sinit, 1)

            for f, dst in ffns:
                t = f if f < NT else f - NT
                inst = inst_of(f)
                if f >= NT and mode == "full":
                    if f == NT:
                        s.wait_ge(s_spill, 256 * NT)
                    for k in range(NK):
                        idx = NB * t + k
                        if t > 0:
                            s.wait_ge(s_vevac, 32 * (NT + t - 1) + 2 * (k + 1))
                        if idx >= 2:
                            s.wait_ge(s_gx, NB * NT + 2 * (idx - 1))
                        s.dma_start(out=x1b[:, k * TT:(k + 1) * TT],
                                    in_=x2p_d.ap()[t, k]).then_inc(s_x2rP[k % 2], 16)
                s.wait_ge(s_vrow, 4 * inst + 2)
                s.copy(out=mbf[0:1, :], in_=rows[0:1, 0:TT]).then_inc(s_smbf, 1)
                s.wait_ge(s_vfin, inst + 1)
                s.activation(out=rows[0:1, 2 * TT:3 * TT], in_=rows[0:1, TT:2 * TT],
                             func=AF.Ln, bias=epsrow[0:1, 0:1], scale=1.0)
                s.activation(out=rows[0:1, TT:2 * TT], in_=rows[0:1, 2 * TT:3 * TT],
                             func=AF.Exp, bias=0.0, scale=-0.5).then_inc(s_sfin, 1)
                if f == 0:
                    s.wait_ge(s_tsetup, 1)
                    s.copy(out=asumn[:], in_=pool[2][0:1, 0:128]).then_inc(s_vsetup, 1)
                    s.wait_ge(s_tsetup, 2)
                    s.copy(out=rsumn[:], in_=pool[3][0:1, 0:8]).then_inc(s_vsetup, 1)
                    for b in range(NB):
                        s.wait_ge(s_wsum, b + 1)
                        s.copy(out=wsumn[0:1, b * 128:(b + 1) * 128],
                               in_=pool[4 + (b % 2)][0:1, 0:128]).then_inc(s_vwsum, 1)
                for c8 in range(8):
                    s.wait_ge(s_vrx, 8 * f + c8 + 1)
                    s.activation(out=tmp_e[:, 8 * c8:8 * c8 + 8],
                                 in_=pool[4][:, 8 * c8:8 * c8 + 8],
                                 func=AF.Exp, bias=rmaxc[:, c8:c8 + 1],
                                 scale=1.0).then_inc(s_srx, 1)
                s.wait_ge(s_tbc1, f + 1)
                s.copy(out=rsBh[:, 0:512], in_=pool[0][:, 0:512]).then_inc(s_sbc1, 1)
                s.copy(out=rsBh[:, 512:TT], in_=pool[1][:, 0:512]).then_inc(s_sbc1, 1)
                s.wait_ge(s_ttp, f + 1)
                s.copy(out=ohT[:], in_=ps_tp[:]).then_inc(s_soh, 1)
                s.wait_ge(s_tmask, f + 1)
                s.copy(out=maskbf[:, 0:512], in_=pool[4][:, 0:512]).then_inc(s_smask, 1)
                s.copy(out=maskbf[:, 512:TT], in_=pool[5][:, 0:512]).then_inc(s_smask, 1)
                if f < NT and t in gate_tiles:
                    inst2 = 2 * t + 1
                    s.sem_inc(s_smbf, 1)
                    s.wait_ge(s_vfin, inst2 + 1)
                    s.activation(out=rows[0:1, 2 * TT:3 * TT], in_=rows[0:1, TT:2 * TT],
                                 func=AF.Ln, bias=epsrow[0:1, 0:1], scale=1.0)
                    s.activation(out=rows[0:1, TT:2 * TT], in_=rows[0:1, 2 * TT:3 * TT],
                                 func=AF.Exp, bias=0.0, scale=-0.5).then_inc(s_sfin, 1)
                    s.wait_ge(s_tbc2, t + 1)
                    s.copy(out=mB[:, 0:512], in_=pool[0][:, 0:512]).then_inc(s_sbc2, 1)
                    s.copy(out=mB[:, 512:TT], in_=pool[1][:, 0:512]).then_inc(s_sbc2, 1)
                    s.copy(out=rsB[:, 0:512], in_=pool[2][:, 0:512]).then_inc(s_sbc2, 1)
                    s.copy(out=rsB[:, 512:TT], in_=pool[3][:, 0:512]).then_inc(s_sbc2, 1)
                    for b in range(NB):
                        bb = NB * t + b
                        if bb > 0:
                            s.wait_ge(s_vy, 2 * bb)
                            s.wait_ge(s_gx, bb)
                        s.wait_ge(s_tgate, 32 * t + 2 * b + 1)
                        for h in range(2):
                            aa = s.activation(out=S[0][:, h * 512:(h + 1) * 512],
                                              in_=pool[(4 * b + h) % 6][:, 0:512],
                                              func=AF.Exp, bias=negba[:, b:b + 1], scale=-1.0)
                        aa.then_inc(s_sevac, 1)
                        s.activation(out=S[1][:, 0:TT], in_=S[0][:, 0:TT],
                                     func=AF.Ln, bias=1.0, scale=1.0)
                        s.activation(out=S[0][:, 0:TT], in_=S[1][:, 0:TT],
                                     func=AF.Exp, bias=0.0, scale=-1.0).then_inc(s_sr, 1)
                        s.wait_ge(s_tgate, 32 * t + 2 * b + 2)
                        for h in range(2):
                            aa = s.activation(out=S[3][:, h * 512:(h + 1) * 512],
                                              in_=pool[(4 * b + 2 + h) % 6][:, 0:512],
                                              func=AF.Exp, bias=negbx[:, b:b + 1], scale=-1.0)
                        aa.then_inc(s_sevac, 1)
                        s.activation(out=S[4][:, 0:TT], in_=S[3][:, 0:TT],
                                     func=AF.Ln, bias=1.0, scale=1.0)
                        s.activation(out=S[3][:, 0:TT], in_=S[4][:, 0:TT],
                                     func=AF.Exp, bias=0.0, scale=-1.0).then_inc(s_si, 1)
                        s.wait_ge(s_vla, bb + 1)
                        s.activation(out=S[2][:, 0:TT], in_=S[1][:, 0:TT],
                                     func=AF.Exp, bias=0.0, scale=1.0).then_inc(s_sa, 1)
                        s.activation(out=S[0][:, 0:TT], in_=S[1][:, 0:TT],
                                     func=AF.Exp, bias=0.0, scale=2.0)
                        s.activation(out=S[5][:, 0:TT], in_=S[0][:, 0:TT],
                                     func=AF.Ln, bias=1.0, scale=-1.0)
                        s.activation(out=S[0][:, 0:TT], in_=S[5][:, 0:TT],
                                     func=AF.Exp, bias=0.0, scale=0.5).then_inc(s_sa, 1)
                        s.wait_ge(s_vy, 2 * (bb + 1))
                        s.copy(out=ysave[:, b:b + 1], in_=S[4][:, TT - 1:TT]).then_inc(s_ysv, 1)
                        s.copy(out=csave[:, b:b + 1], in_=S[3][:, TT - 1:TT]).then_inc(s_ysv, 1)
                        if bb >= 2:
                            s.wait_ge(s_pspill, 16 * (bb - 1))
                        s.activation(out=pstage[:, (b % 2) * TT:(b % 2 + 1) * TT],
                                     in_=S[3][:, 0:TT], func=AF.Exp,
                                     bias=0.0, scale=1.0).then_inc(s_sp9, 1)

        @block.vector
        def _(v):
            v.wait_ge(s_giota, 1)
            v.wait_ge(s_ld, 64)
            if dbgf_d is not None:
                v.memset(topguard[:, 0:1024], 123.25)
            v.memset(onesb[:], -1.0).then_inc(s_vinit, 1)
            v.memset(onesf[:], 1.0).then_inc(s_vinit, 1)
            v.memset(epsrow[:], EPS).then_inc(s_vinit, 1)
            v.tensor_scalar(out=idbf[:], in0=id_i[:], scalar1=0, scalar2=None,
                            op0=ALU.is_equal).then_inc(s_vinit, 1)
            v.tensor_scalar(out=exbf[:], in0=ex_i[:], scalar1=0, scalar2=2.0,
                            op0=ALU.is_equal, op1=ALU.mult).then_inc(s_vinit, 1)
            v.tensor_scalar(out=negba[:], in0=bload[:, NB:2 * NB], scalar1=-1.0,
                            scalar2=None, op0=ALU.mult).then_inc(s_vinit, 1)
            v.tensor_scalar(out=negbx[:], in0=bload[:, 2 * NB:3 * NB], scalar1=-1.0,
                            scalar2=None, op0=ALU.mult).then_inc(s_vinit, 1)
            v.tensor_scalar(out=hbrow[:], in0=bload[:, 0:NB], scalar1=0.5,
                            scalar2=None, op0=ALU.mult).then_inc(s_vinit, 1)
            v.wait_ge(s_sinit, 2)
            v.tensor_scalar(out=k1h[:], in0=ksc[:, NB:2 * NB], scalar1=-8.0,
                            scalar2=None, op0=ALU.mult)

            def stats_partials():
                for k in range(NK):
                    xk = xbf[:, k * TT:(k + 1) * TT]
                    v.tensor_tensor(out=bsq[:, 0:TT], in0=xk, in1=xk, op=ALU.mult)
                    if k == 0:
                        v.tensor_copy(bacc2[:, 0:TT], bsq[:, 0:TT])
                    else:
                        last = v.tensor_tensor(out=bacc2[:, 0:TT], in0=bacc2[:, 0:TT],
                                               in1=bsq[:, 0:TT], op=ALU.add)
                        if k == NK - 2:
                            last.then_inc(s_vst, 1)
                return last

            def ln_rows_and_finish(inst, is_ffn, f):
                rb = 4 * inst
                for j in range(4):
                    h = j % 2
                    if j == 0 and inst > 0:
                        v.wait_ge(s_smbf, inst)
                    v.wait_ge(s_trow, rb + j + 1)
                    dstoff = (0 if j < 2 else TT) + h * 512
                    v.tensor_scalar(out=rows[0:1, dstoff:dstoff + 512],
                                    in0=ps_row[0:1, 0:512], scalar1=-INV_D,
                                    scalar2=None, op0=ALU.mult).then_inc(s_vrow, 1)
                v.tensor_tensor(out=rows[0:1, 2 * TT:3 * TT], in0=rows[0:1, 0:TT],
                                in1=rows[0:1, 0:TT], op=ALU.mult)
                v.tensor_tensor(out=rows[0:1, TT:2 * TT], in0=rows[0:1, TT:2 * TT],
                                in1=rows[0:1, 2 * TT:3 * TT],
                                op=ALU.subtract).then_inc(s_vfin, 1)
                if is_ffn:
                    v.wait_ge(s_sfin, inst + 1)
                    v.tensor_scalar(out=rows[0:1, 2 * TT:3 * TT],
                                    in0=rows[0:1, TT:2 * TT], scalar1=0.5,
                                    scalar2=None, op0=ALU.mult).then_inc(s_vrsh, 1)

            for f, dst in ffns:
                t = f if f < NT else f - NT
                inst = inst_of(f)
                if f < NT:
                    v.wait_ge(s_xbf, 16 * (t + 1))
                    if inst > 0:
                        v.wait_ge(s_trow, 4 * inst)
                    stats_partials().then_inc(s_vst, 1)
                else:
                    if f == NT:
                        v.wait_ge(s_hfin, 1)
                        v.wait_ge(s_tgate, 32 * NT)
                    v.wait_ge(s_trow, 4 * inst)
                    for k in range(NK):
                        v.wait_ge(s_x2rP[k % 2], 16 * (t * 8 + k // 2 + 1))
                        v.wait_ge(s_pldP[k % 2], 16 * ((NB * t + k) // 2 + 1))
                        v.scalar_tensor_tensor(out=x1b[:, k * TT:(k + 1) * TT],
                                               in0=pstage[:, (k % 2) * TT:(k % 2 + 1) * TT],
                                               scalar=hfin[:, k:k + 1],
                                               in1=x1b[:, k * TT:(k + 1) * TT],
                                               op0=ALU.mult, op1=ALU.add).then_inc(s_gx, 1)
                        v.tensor_scalar(out=xbf[:, k * TT:(k + 1) * TT],
                                        in0=x1b[:, k * TT:(k + 1) * TT],
                                        scalar1=1.0, scalar2=None,
                                        op0=ALU.mult).then_inc(s_gx, 1)
                    stats_partials().then_inc(s_vst, 1)
                ln_rows_and_finish(inst, True, f)
                v.wait_ge(s_trt, f + 1)
                for c8 in range(8):
                    v.tensor_reduce(out=rmaxc[:, c8:c8 + 1], in_=pool[4][:, 8 * c8:8 * c8 + 8],
                                    axis=AX.X, op=ALU.max, negate=True).then_inc(s_vrx, 1)
                v.wait_ge(s_srx, 8 * (f + 1))
                v.tensor_scalar(out=oh_sb[:, 0:64], in0=tmp_e[:, 0:64],
                                scalar1=1.0, scalar2=None,
                                op0=ALU.is_ge).then_inc(s_voh, 1)
                v.wait_ge(s_tA, 2 * f + 2)
                v.wait_ge(s_smask, 2 * (f + 1))
                for h in range(2):
                    v.tensor_tensor(out=hbf[:, h * 512:(h + 1) * 512],
                                    in0=pool[2 + h][:, 0:512],
                                    in1=maskbf[:, h * 512:(h + 1) * 512],
                                    op=ALU.mult).then_inc(s_vhbf, 1)
                v.wait_ge(s_sbc1, 2 * (f + 1))
                if f < NT:
                    v.wait_ge(s_x0, 16 * (f + 1))
                for u in range(32):
                    b, h = u // 2, u % 2
                    bank = u % 6
                    v.wait_ge(s_tbase, 32 * f + u + 1)
                    if dst == "out" and (NK * t + b) >= 2 and h == 0:
                        v.wait_ge(s_out, 16 * (NK * t + b))
                    te = tmp_e[:, 0:512]
                    v.tensor_tensor(out=te, in0=pool[bank][:, 0:512],
                                    in1=rsBh[:, h * 512:(h + 1) * 512], op=ALU.mult)
                    x1sl = x1b[:, b * TT + h * 512:b * TT + (h + 1) * 512]
                    if dst == "x1":
                        v.scalar_tensor_tensor(out=x1sl, in0=te, scalar=hbrow[:, b:b + 1],
                                               in1=x1sl, op0=ALU.add,
                                               op1=ALU.add).then_inc(s_vevac, 1)
                        if t in gate_tiles:
                            v.tensor_tensor(out=bsq[:, h * 512:(h + 1) * 512], in0=x1sl,
                                            in1=x1sl, op=ALU.mult)
                            if b == 0:
                                v.tensor_scalar(out=bacc[:, h * 512:(h + 1) * 512], in0=x1sl,
                                                scalar1=1.0, scalar2=None, op0=ALU.mult)
                                v.tensor_copy(bacc2[:, h * 512:(h + 1) * 512],
                                              bsq[:, h * 512:(h + 1) * 512])
                            else:
                                acci = v.tensor_tensor(out=bacc[:, h * 512:(h + 1) * 512],
                                                in0=bacc[:, h * 512:(h + 1) * 512],
                                                in1=x1sl, op=ALU.add)
                                acc2i = v.tensor_tensor(out=bacc2[:, h * 512:(h + 1) * 512],
                                                in0=bacc2[:, h * 512:(h + 1) * 512],
                                                in1=bsq[:, h * 512:(h + 1) * 512], op=ALU.add)
                                if u == 31:
                                    acci.then_inc(s_vst, 1)
                                    acc2i.then_inc(s_vst, 1)
                    else:
                        stg = S[b % 2][:, h * 512:(h + 1) * 512]
                        v.scalar_tensor_tensor(out=stg, in0=te, scalar=hbrow[:, b:b + 1],
                                               in1=x1sl, op0=ALU.add,
                                               op1=ALU.add).then_inc(s_vevac, 1)
                if f < NT and t in gate_tiles:
                    inst2 = 2 * t + 1
                    ln_rows_and_finish(inst2, False, f)
                    v.wait_ge(s_sbc2, 4 * (t + 1))
                    for h in range(2):
                        for b in range(NB):
                            x1sl = x1b[:, b * TT + h * 512:b * TT + (h + 1) * 512]
                            te = tmp_e[:, 0:512]
                            v.tensor_tensor(out=te, in0=x1sl,
                                            in1=mB[:, h * 512:(h + 1) * 512], op=ALU.subtract)
                            v.tensor_tensor(out=xbf[:, b * TT + h * 512:b * TT + (h + 1) * 512],
                                            in0=te, in1=rsB[:, h * 512:(h + 1) * 512],
                                            op=ALU.mult).then_inc(s_vap, 1)
                    for b in range(NB):
                        bb = NB * t + b
                        v.wait_ge(s_sr, bb + 1)
                        v.tensor_scalar(out=S[1][:, 0:TT], in0=S[0][:, 0:TT],
                                        scalar1=k1h[:, b:b + 1], scalar2=LACLAMP,
                                        op0=ALU.mult, op1=ALU.min).then_inc(s_vla, 1)
                        v.wait_ge(s_sa, 2 * (bb + 1))
                        v.wait_ge(s_si, bb + 1)
                        x1sl = x1b[:, b * TT:(b + 1) * TT]
                        v.tensor_tensor(out=S[4][:, 0:TT], in0=x1sl, in1=mB[:, 0:TT],
                                        op=ALU.subtract)
                        v.tensor_tensor(out=S[5][:, 0:TT], in0=S[4][:, 0:TT],
                                        in1=rsB[:, 0:TT], op=ALU.mult)
                        v.tensor_tensor(out=S[4][:, 0:TT], in0=S[3][:, 0:TT],
                                        in1=S[5][:, 0:TT], op=ALU.mult)
                        v.tensor_tensor(out=S[5][:, 0:TT], in0=S[4][:, 0:TT],
                                        in1=S[0][:, 0:TT], op=ALU.mult)
                        if t > 0:
                            v.wait_ge(s_ysv, 2 * (NB * (t - 1) + b + 1))
                        init_y = 0.0 if t == 0 else ysave[:, b:b + 1]
                        v.tensor_tensor_scan(out=S[4][:, 0:TT], data0=S[2][:, 0:TT],
                                             data1=S[5][:, 0:TT], initial=init_y,
                                             op0=ALU.mult, op1=ALU.add).then_inc(s_vy, 1)
                        init_c = 0.0 if t == 0 else csave[:, b:b + 1]
                        v.tensor_tensor_scan(out=S[3][:, 0:TT], data0=S[1][:, 0:TT],
                                             data1=S[1][:, 0:TT], initial=init_c,
                                             op0=ALU.add, op1=ALU.bypass).then_inc(s_vy, 1)

    return nc


def kernel(L0_attn_x, L0_attn_Wa, L0_attn_Wx, L0_attn_ba, L0_attn_bx,
           L0_attn_Lam, L0_ffn_W_base, L0_ffn_b_base, L0_ffn_router_W,
           L0_ffn_A, L0_ffn_Bw):
    mode = _MODE
    B = 4

    xT = np.ascontiguousarray(np.asarray(L0_attn_x, np.float32).transpose(0, 2, 1))

    def lhsT_blocks(W):
        return np.ascontiguousarray(
            np.asarray(W, np.float32).T.reshape(D, NB, 128).transpose(1, 0, 2))

    wb = lhsT_blocks(L0_ffn_W_base)
    wa = lhsT_blocks(L0_attn_Wa)
    wx = lhsT_blocks(L0_attn_Wx)
    rt = np.ascontiguousarray(np.asarray(L0_ffn_router_W, np.float32).T)
    at = np.ascontiguousarray(np.asarray(L0_ffn_A, np.float32).reshape(128, D).T)
    bwt = np.ascontiguousarray(
        np.asarray(L0_ffn_Bw, np.float32).transpose(0, 2, 1).reshape(128, D))

    def col16(vec):
        return np.ascontiguousarray(np.asarray(vec, np.float32).reshape(NB, 128).T)

    bvec = np.ascontiguousarray(np.concatenate(
        [col16(L0_ffn_b_base), col16(L0_attn_ba),
         col16(L0_attn_bx), col16(L0_attn_Lam)], axis=1))

    nonce = np.full((1, 8), int(time.time_ns() // 1000) % (2**30) + 1, np.int32)

    nc = build_nc(mode)
    in_maps = []
    for c in range(8):
        b, half = c // 2, c % 2
        in_maps.append(dict(
            x=np.ascontiguousarray(xT[b, :, half * TPC:(half + 1) * TPC]),
            wb=wb, wa=wa, wx=wx, rt=rt, at=at, bwt=bwt, bvec=bvec,
            hmask=np.full((128, 1), float(half), np.float32),
            slot=np.array([[half]], np.int32),
            nonce=nonce,
        ))
    res = run_bass_kernel_spmd(nc, in_maps, core_ids=list(range(8)))

    outT = np.empty((B, D, 2 * TPC), np.float32)
    for c in range(8):
        b, half = c // 2, c % 2
        o = res.results[c]["out"]
        o = np.asarray(o).transpose(1, 2, 0, 3).reshape(D, TPC)
        outT[b, :, half * TPC:(half + 1) * TPC] = o
    return np.ascontiguousarray(outT.transpose(0, 2, 1))


# version 28
# speedup vs baseline: 1.4083x; 1.4083x over previous
import time
import numpy as np
import ml_dtypes
from contextlib import ExitStack

import concourse.bass as bass
import concourse.mybir as mybir
from concourse.bass_utils import run_bass_kernel_spmd

F32 = mybir.dt.float32
BF16 = mybir.dt.bfloat16
I32 = mybir.dt.int32
AF = mybir.ActivationFunctionType
ALU = mybir.AluOpType
AX = mybir.AxisListType

D = 2048
TPC = 4096
TT = 1024
NT = TPC // TT
NB = 16
NK = 16
EPS = 1e-5
LACLAMP = -2e-6
INV_D = 1.0 / D

_MODE = "full"


def build_nc(mode):
    nc = bass.Bass("TRN2", target_bir_lowering=False)

    x_in = nc.dram_tensor("x", [D, TPC], F32, kind="ExternalInput").ap()
    wb_in = nc.dram_tensor("wb", [NB, D, 128], F32, kind="ExternalInput").ap()
    wa_in = nc.dram_tensor("wa", [NB, D, 128], F32, kind="ExternalInput").ap()
    wx_in = nc.dram_tensor("wx", [NB, D, 128], F32, kind="ExternalInput").ap()
    rt_in = nc.dram_tensor("rt", [D, 8], F32, kind="ExternalInput").ap()
    at_in = nc.dram_tensor("at", [D, 128], F32, kind="ExternalInput").ap()
    bwt_in = nc.dram_tensor("bwt", [128, D], F32, kind="ExternalInput").ap()
    bvec_in = nc.dram_tensor("bvec", [128, 4 * NB], F32, kind="ExternalInput").ap()
    hmask_in = nc.dram_tensor("hmask", [128, 1], F32, kind="ExternalInput").ap()
    slot_in = nc.dram_tensor("slot", [1, 1], I32, kind="ExternalInput").ap()
    nonce_in = nc.dram_tensor("nonce", [1, 8], I32, kind="ExternalInput").ap()

    out_d = nc.dram_tensor("out", [NT, NB, 128, TT], F32, kind="ExternalOutput").ap()
    dbg_d = None
    if mode == "x2p":
        dbg_d = nc.dram_tensor("dbg", [NT, 8, 128, TT], F32, kind="ExternalOutput").ap()
    dbgf_d = None
    if mode == "ffn1":
        dbgf_d = nc.dram_tensor("dbgf", [20, 2048], F32, kind="ExternalOutput").ap()

    wbb = nc.dram_tensor("wbb", [NB, 128, D], BF16, kind="Internal")
    wab = nc.dram_tensor("wab", [NB, 128, D], BF16, kind="Internal")
    wxb = nc.dram_tensor("wxb", [NB, 128, D], BF16, kind="Internal")
    x2p_d = nc.dram_tensor("x2pd", [NT, NB, 128, TT], F32, kind="Internal")
    p_d = nc.dram_tensor("pdd", [NT, NB, 128, TT], BF16, kind="Internal")
    ex_d = nc.dram_tensor("exd", [2, 128, NB], F32, kind="Internal", addr_space="Shared")
    fl_d = nc.dram_tensor("fld", [2, 1, 8], I32, kind="Internal", addr_space="Shared")

    ctx = ExitStack()
    with ctx:
        def sbt(name, shape, dt=F32):
            return ctx.enter_context(nc.sbuf_tensor(name, shape, dt))

        def sem(name):
            return ctx.enter_context(nc.semaphore(name))

        x1b = sbt("x1b", [128, NB * TT])
        xbf = sbt("xbf", [128, NB * TT], BF16)
        wpool = sbt("wpool", [128, 6 * D], BF16)
        sbAT = sbt("sbAT", [128, D], BF16)
        sbBwT = sbt("sbBwT", [128, D], BF16)
        sbRt = sbt("sbRt", [128, NK * 8], BF16)
        S = [sbt(f"gs{i}", [128, TT]) for i in range(6)]
        tmp_e = sbt("tmp_e", [128, 512])
        bacc = sbt("bacc", [128, TT], BF16)
        bacc2 = sbt("bacc2", [128, TT], BF16)
        bsq = sbt("bsq", [128, TT], BF16)
        mB = sbt("mB", [128, TT])
        rsB = sbt("rsB", [128, TT])
        rsBh = sbt("rsBh", [128, TT])
        rows = sbt("rows", [1, 3 * TT])
        mbf = sbt("mbf", [1, TT], BF16)
        maskbf = sbt("maskbf", [128, TT], BF16)
        hbf = sbt("hbf", [128, TT], BF16)
        oh_sb = sbt("oh_sb", [128, 64], BF16)
        ohT = sbt("ohT", [8, TT], BF16)
        pstage = sbt("pstage", [128, 2 * TT], BF16)
        ysave = sbt("ysave", [128, NB])
        csave = sbt("csave", [128, NB])
        bload = sbt("bload", [128, 4 * NB])
        k1h = sbt("k1h", [128, NB])
        negba = sbt("negba", [128, NB])
        negbx = sbt("negbx", [128, NB])
        hbrow = sbt("hbrow", [128, NB])
        ksc = sbt("ksc", [128, 2 * NB])
        wsumn = sbt("wsumn", [1, D], BF16)
        asumn = sbt("asumn", [1, 128], BF16)
        rsumn = sbt("rsumn", [1, 8], BF16)
        onesb = sbt("onesb", [128, 1], BF16)
        onesf = sbt("onesf", [1, 128])
        id_i = sbt("id_i", [128, 128], I32)
        idbf = sbt("idbf", [128, 128], BF16)
        ex_i = sbt("ex_i", [8, 128], I32)
        exbf = sbt("exbf", [8, 128], BF16)
        hraw = sbt("hraw", [128, NB])
        hfin = sbt("hfin", [128, NB])
        hmk = sbt("hmk", [128, 1])
        sbslot = sbt("sbslot", [1, 1], I32)
        sbnon = sbt("sbnon", [1, 8], I32)
        sbfl = sbt("sbfl", [1, 16], I32)
        epsrow = sbt("epsrow", [1, 1])
        rmaxc = sbt("rmaxc", [128, 8])
        topguard = sbt("topguard", [128, 1024])

        pool = [ctx.enter_context(nc.psum_tensor(f"pp{i}", [128, 512], F32)) for i in range(6)]
        ps_row = ctx.enter_context(nc.psum_tensor("ps_row", [1, 512], F32))
        ps_tp = ctx.enter_context(nc.psum_tensor("ps_tp", [8, TT], BF16))

        s_ld = sem("s_ld");  s_cast = sem("s_cast");  s_giota = sem("s_giota")
        s_vinit = sem("s_vinit");  s_sinit = sem("s_sinit")
        s_tsetup = sem("s_tsetup");  s_vsetup = sem("s_vsetup")
        s_x0 = sem("s_x0");  s_xbf = sem("s_xbf")
        s_wb = sem("s_wb");  s_wg = sem("s_wg")
        s_vst = sem("s_vst");  s_trow = sem("s_trow");  s_vrow = sem("s_vrow")
        s_vfin = sem("s_vfin");  s_sfin = sem("s_sfin");  s_vrsh = sem("s_vrsh")
        s_tbc1 = sem("s_tbc1");  s_sbc1 = sem("s_sbc1")
        s_tbc2 = sem("s_tbc2");  s_sbc2 = sem("s_sbc2")
        s_trt = sem("s_trt");  s_voh = sem("s_voh");  s_ttp = sem("s_ttp")
        s_soh = sem("s_soh");  s_tmask = sem("s_tmask");  s_smask = sem("s_smask")
        s_tA = sem("s_tA");  s_vhbf = sem("s_vhbf")
        s_tbase = sem("s_tbase");  s_vevac = sem("s_vevac")
        s_vap = sem("s_vap");  s_tgate = sem("s_tgate");  s_sevac = sem("s_sevac")
        s_sr = sem("s_sr");  s_si = sem("s_si");  s_vla = sem("s_vla")
        s_sa = sem("s_sa");  s_vy = sem("s_vy");  s_sp9 = sem("s_sp9")
        s_gx = sem("s_gx");  s_spill = sem("s_spill");  s_pspill = sem("s_pspill")
        s_pld = sem("s_pld");  s_exch = sem("s_exch");  s_poll = sem("s_poll")
        s_hfin = sem("s_hfin");  s_out = sem("s_out")
        s_wsum = sem("s_wsum");  s_vwsum = sem("s_vwsum");  s_ysv = sem("s_ysv")
        s_vrx = sem("s_vrx");  s_srx = sem("s_srx");  s_wsld = sem("s_wsld")
        s_smbf = sem("s_smbf");  s_ydbg = sem("s_ydbg")
        s_cwb = sem("s_cwb");  s_cmisc = sem("s_cmisc")
        s_cwa = sem("s_cwa");  s_cwx = sem("s_cwx")
        s_wsldP = [sem("s_wsldA"), sem("s_wsldB")]
        s_wbP = [sem("s_wbA"), sem("s_wbB")]
        s_wgaP = [sem("s_wgaA"), sem("s_wgaB")]
        s_wgxP = [sem("s_wgxA"), sem("s_wgxB")]
        s_pldP = [sem("s_pldA"), sem("s_pldB")]
        s_x2rP = [sem("s_x2rA"), sem("s_x2rB")]
        s_wbP4 = [sem(f"s_wb4{i}") for i in range(4)]

        block = ctx.enter_context(nc.Block())

        if mode == "ffn1":
            ffns = [(f, "out") for f in range(NT)]
            gate_tiles = []
        elif mode == "x2p":
            ffns = [(f, "x1") for f in range(NT)]
            gate_tiles = list(range(NT))
        else:
            ffns = [(f, "x1") for f in range(NT)] + [(f, "out") for f in range(NT, 2 * NT)]
            gate_tiles = list(range(NT))

        def inst_of(f):
            if mode == "ffn1":
                return f
            if f < NT:
                return 2 * f
            return 2 * NT + (f - NT)

        x2p_spill = out_d if mode == "x2p" else x2p_d.ap()

        @block.sync
        def _(y):
            y.dma_start(out=bload[:], in_=bvec_in[:]).then_inc(s_ld, 16)
            y.dma_start(out=hmk[:], in_=hmask_in[:]).then_inc(s_ld, 16)
            y.dma_start(out=sbslot[:], in_=slot_in[:]).then_inc(s_ld, 16)
            y.dma_start(out=sbnon[:], in_=nonce_in[:]).then_inc(s_ld, 16)
            if dbgf_d is not None:
                y.wait_ge(s_vsetup, 1)
                y.dma_start(out=dbgf_d[16:17, 0:64],
                            in_=asumn[0:1, :].bitcast(F32)).then_inc(s_ydbg, 16)
                y.wait_ge(s_vsetup, 2)
                y.dma_start(out=dbgf_d[16:17, 64:68],
                            in_=rsumn[0:1, :].bitcast(F32)).then_inc(s_ydbg, 16)
            y.wait_ge(s_cwb, 256)
            for b in range(NB):
                if b >= 2:
                    y.wait_ge(s_wsum, b - 1)
                y.dma_start(out=wpool[:, (b % 2) * D:(b % 2 + 1) * D],
                            in_=wbb.ap()[b]).then_inc(s_wsldP[b % 2], 16)

            for f, dst in ffns:
                t = f if f < NT else f - NT
                if f < NT:
                    if t > 0:
                        if gate_tiles:
                            y.wait_ge(s_spill, 256 * t)
                        else:
                            y.wait_ge(s_vevac, 32 * t)
                    y.dma_start(
                        out=x1b[:].rearrange("p (k t) -> p k t", k=NK),
                        in_=x_in[:, t * TT:(t + 1) * TT].rearrange("(k p) t -> p k t", p=128),
                    ).then_inc(s_x0, 16)
                for b in range(NB):
                    idx = NK * f + b
                    if f >= NT:
                        if f == NT and b in (2, 3):
                            y.wait_ge(s_tgate, 32 * NT)
                        if idx >= 4:
                            y.wait_ge(s_tbase, 2 * (idx - 3))
                        y.dma_start(
                            out=wpool[:, (idx % 4) * D:(idx % 4 + 1) * D],
                            in_=wbb.ap()[b],
                        ).then_inc(s_wbP4[idx % 4], 16)
                    else:
                        if idx >= 2:
                            y.wait_ge(s_tbase, 2 * (idx - 1))
                        elif idx == 0:
                            y.wait_ge(s_wsum, NB - 1)
                        else:
                            y.wait_ge(s_wsum, NB)
                        y.dma_start(
                            out=wpool[:, (idx % 2) * D:(idx % 2 + 1) * D],
                            in_=wbb.ap()[b],
                        ).then_inc(s_wbP[idx % 2], 16)
                    if dst == "out" and b >= 2:
                        bo = b - 2
                        y.wait_ge(s_vevac, 32 * f + 2 * (bo + 1))
                        y.dma_start(out=out_d[t, bo], in_=S[bo % 2][:, 0:TT]).then_inc(s_out, 16)
                if dst == "out":
                    for bo in (NB - 2, NB - 1):
                        y.wait_ge(s_vevac, 32 * f + 2 * (bo + 1))
                        y.dma_start(out=out_d[t, bo], in_=S[bo % 2][:, 0:TT]).then_inc(s_out, 16)
                if f < NT and t in gate_tiles:
                    for b in range(NB):
                        if b >= 2:
                            y.wait_ge(s_tgate, 32 * t + 2 * (b - 2) + 2)
                        elif t > 0:
                            y.wait_ge(s_tgate, 32 * (t - 1) + 2 * (NB + b - 2) + 2)
                        if t == 0 and b == 0:
                            y.wait_ge(s_cwa, 256)
                            y.wait_ge(s_cwx, 256)
                        y.dma_start(
                            out=wpool[:, (2 + b % 2) * D:(2 + b % 2 + 1) * D],
                            in_=wab.ap()[b],
                        ).then_inc(s_wgaP[b % 2], 16)
                        y.dma_start(
                            out=wpool[:, (4 + b % 2) * D:(4 + b % 2 + 1) * D],
                            in_=wxb.ap()[b],
                        ).then_inc(s_wgxP[b % 2], 16)
            if mode == "x2p":
                y.wait_ge(s_spill, 256 * NT)
                y.wait_ge(s_pspill, 256 * NT)
            else:
                y.wait_ge(s_out, 16 * NB * NT)

        @block.gpsimd
        def _(g):
            g.iota(id_i[:], pattern=[[1, 128]], base=0, channel_multiplier=-1)
            g.iota(ex_i[:], pattern=[[1, 8], [0, 16]], base=0,
                   channel_multiplier=-1).then_inc(s_giota, 1)
            g.dma_start(
                out=xbf[:].rearrange("p (k t) -> p k t", k=NK),
                in_=x_in[:, 0:TT].rearrange("(k p) t -> p k t", p=128),
            ).then_inc(s_xbf, 16)
            for b in range(NB):
                g.dma_start(out=wbb.ap()[b].rearrange("p (k o) -> p k o", k=NK),
                            in_=wb_in[b].rearrange("(k p) o -> p k o", p=128)).then_inc(s_cwb, 16)
            g.dma_start(out=sbAT[:].rearrange("p (k e) -> p k e", k=NK),
                        in_=at_in.rearrange("(k p) e -> p k e", p=128)).then_inc(s_cmisc, 16)
            g.dma_start(out=sbBwT[:], in_=bwt_in[:]).then_inc(s_cmisc, 16)
            g.dma_start(out=sbRt[:].rearrange("p (k e) -> p k e", k=NK),
                        in_=rt_in.rearrange("(k p) e -> p k e", p=128)).then_inc(s_cmisc, 16)
            for b in range(NB):
                g.dma_start(out=wab.ap()[b].rearrange("p (k o) -> p k o", k=NK),
                            in_=wa_in[b].rearrange("(k p) o -> p k o", p=128)).then_inc(s_cwa, 16)
            for b in range(NB):
                g.dma_start(out=wxb.ap()[b].rearrange("p (k o) -> p k o", k=NK),
                            in_=wx_in[b].rearrange("(k p) o -> p k o", p=128)).then_inc(s_cwx, 16)

            if dbgf_d is not None:
                g.wait_ge(s_vwsum, NB)
                g.dma_start(out=dbgf_d[0:1, :], in_=wsumn[0:1, :]).then_inc(s_poll, 16)
                g.wait_ge(s_vsetup, 2)
                g.dma_start(out=dbgf_d[1:2, 0:128], in_=asumn[0:1, :]).then_inc(s_poll, 16)
                g.dma_start(out=dbgf_d[1:2, 128:136], in_=rsumn[0:1, :]).then_inc(s_poll, 16)
                g.wait_ge(s_vinit, 8)
                g.dma_start(out=dbgf_d[2:3, :].rearrange("a (p c) -> (a p) c", p=128),
                            in_=hbrow[:, 0:NB]).then_inc(s_poll, 16)
                g.dma_start(out=dbgf_d[3:4, 0:1024],
                            in_=wsumn[0:1, :].bitcast(F32)).then_inc(s_poll, 16)
                g.dma_start(out=dbgf_d[3:4, 1024:1088],
                            in_=asumn[0:1, :].bitcast(F32)).then_inc(s_poll, 16)
                g.dma_start(out=dbgf_d[3:4, 1088:1092],
                            in_=rsumn[0:1, :].bitcast(F32)).then_inc(s_poll, 16)
            for t in range(NT):
                if t > 0 and not gate_tiles:
                    g.wait_ge(s_tbase, 32 * t)
                    g.dma_start(
                        out=xbf[:].rearrange("p (k t) -> p k t", k=NK),
                        in_=x_in[:, t * TT:(t + 1) * TT].rearrange("(k p) t -> p k t", p=128),
                    ).then_inc(s_xbf, 16)
                if dbgf_d is not None:
                    g.wait_ge(s_sfin, t + 1)
                    g.dma_start(out=dbgf_d[4 + 2 * t:5 + 2 * t, 0:TT],
                                in_=rows[0:1, 0:TT]).then_inc(s_poll, 16)
                    g.dma_start(out=dbgf_d[5 + 2 * t:6 + 2 * t, 0:TT],
                                in_=rows[0:1, TT:2 * TT]).then_inc(s_poll, 16)
                if t not in gate_tiles:
                    continue
                for b in range(NB):
                    g.wait_ge(s_vy, 2 * (NB * t + b) + 1)
                    if dbg_d is not None and b == 0:
                        for di, sl in enumerate((4, 2, 5, 1, 0, 3)):
                            g.dma_start(out=dbg_d[t, di], in_=S[sl][:, 0:TT]).then_inc(s_poll, 16)
                        if t > 0:
                            g.dma_start(out=dbg_d[t, 6][:, 0:NB], in_=ysave[:]).then_inc(s_poll, 16)
                            g.dma_start(out=dbg_d[t, 7][:, 0:NB], in_=csave[:]).then_inc(s_poll, 16)
                        g.wait_ge(s_poll, 128 * (t + 1) - 32)
                    g.tensor_tensor(out=x1b[:, b * TT:(b + 1) * TT],
                                    in0=x1b[:, b * TT:(b + 1) * TT],
                                    in1=S[4][:, 0:TT], op=ALU.add).then_inc(s_gx, 1)
                    g.dma_start(out=x2p_spill[t, b],
                                in_=x1b[:, b * TT:(b + 1) * TT]).then_inc(s_spill, 16)
                    g.wait_ge(s_sp9, NB * t + b + 1)
                    g.dma_start(out=p_d.ap()[t, b],
                                in_=pstage[:, (b % 2) * TT:(b % 2 + 1) * TT]).then_inc(s_pspill, 16)
                    if b == 13 and t + 1 < NT:
                        g.wait_ge(s_tgate, 32 * (t + 1))
                        g.dma_start(
                            out=xbf[:].rearrange("p (k t) -> p k t", k=NK),
                            in_=x_in[:, (t + 1) * TT:(t + 2) * TT].rearrange(
                                "(k p) t -> p k t", p=128),
                        ).then_inc(s_xbf, 16)
            if mode != "full":
                if dbgf_d is not None:
                    g.wait_ge(s_out, 16 * NB * NT)
                    g.dma_start(out=dbgf_d[12:13, 0:128],
                                in_=asumn[0:1, :]).then_inc(s_poll, 16)
                    g.dma_start(out=dbgf_d[12:13, 128:136],
                                in_=rsumn[0:1, :]).then_inc(s_poll, 16)
                    g.dma_start(out=dbgf_d[13:14, 0:1024],
                                in_=wsumn[0:1, :].bitcast(F32)).then_inc(s_poll, 16)
                    g.dma_start(out=dbgf_d[13:14, 1024:1088],
                                in_=asumn[0:1, :].bitcast(F32)).then_inc(s_poll, 16)
                    g.dma_start(out=dbgf_d[13:14, 1088:1092],
                                in_=rsumn[0:1, :].bitcast(F32)).then_inc(s_poll, 16)
                    g.dma_start(out=dbgf_d[14:15, 0:1024],
                                in_=mbf[0:1, :]).then_inc(s_poll, 16)
                    g.dma_start(out=dbgf_d[15:16, 0:512],
                                in_=mbf[0:1, :].bitcast(F32)).then_inc(s_poll, 16)
                    g.dma_start(out=dbgf_d[17:18, 0:1024],
                                in_=topguard[0:1, :]).then_inc(s_poll, 16)
                    g.dma_start(out=dbgf_d[18:19, 0:1024],
                                in_=topguard[127:128, :]).then_inc(s_poll, 16)
                    g.wait_ge(s_poll, 16 * (7 + 2 * NT + 9))
                    g.wait_ge(s_ydbg, 32)
                return
            g.wait_ge(s_ysv, 2 * NB * NT)
            with (
                g.register("slotr") as slotr,
                g.register("nonr") as nonr,
                g.register("more") as more,
                g.register("cnt") as cnt,
                g.register("w") as w,
            ):
                g.wait_ge(s_ld, 64)
                g.reg_load(slotr, sbslot[0:1, 0:1])
                g.reg_load(nonr, sbnon[0:1, 0:1])
                sidx = g.snap(slotr, min_val=0, max_val=1)
                g.dma_start(out=ex_d.ap()[sidx], in_=ysave[:]).then_inc(s_exch, 16)
                g.wait_ge(s_exch, 16)
                g.dma_start(out=fl_d.ap()[sidx], in_=sbnon[:]).then_inc(s_exch, 16)
                g.wait_ge(s_exch, 32)
                g.reg_mov(more, 1)
                with g.While(more):
                    g.dma_start(out=sbfl[:],
                                in_=fl_d.ap().rearrange("a b c -> b (a c)")).then_inc(s_poll, 16)
                    g.drain()
                    g.reg_mov(cnt, 0)
                    for sl in range(2):
                        for k in range(4):
                            g.reg_load(w, sbfl[0:1, 8 * sl + k:8 * sl + k + 1])
                            with g.If_eq(w, nonr):
                                g.reg_add(cnt, cnt, 1)
                    g.reg_mov(more, 1)
                    with g.If_eq(cnt, 8):
                        g.reg_mov(more, 0)
                g.dma_start(out=hraw[:], in_=ex_d.ap()[0]).then_inc(s_exch, 16)
            g.wait_ge(s_exch, 48)
            g.tensor_scalar(out=hfin[:], in0=hraw[:], scalar1=hmk[:, 0:1],
                            scalar2=None, op0=ALU.mult).then_inc(s_hfin, 1)
            g.wait_ge(s_pspill, 256 * NT)
            for t in range(NT):
                for b in range(NB):
                    if NB * t + b >= 2:
                        g.wait_ge(s_gx, NB * NT + 2 * (NB * t + b - 1))
                    g.dma_start(out=pstage[:, (b % 2) * TT:(b % 2 + 1) * TT],
                                in_=p_d.ap()[t, b]).then_inc(s_pldP[b % 2], 16)

        @block.tensor
        def _(t_):
            def setup_sums():
                t_.wait_ge(s_cwb, 256)
                t_.wait_ge(s_cmisc, 48)
                for k in range(NK):
                    t_.matmul(pool[2][0:1, 0:128], lhsT=onesb[:, 0:1],
                              rhs=sbAT[:, k * 128:(k + 1) * 128],
                              start=(k == 0), stop=(k == NK - 1))
                t_.drain().then_inc(s_tsetup, 1)
                for k in range(NK):
                    t_.matmul(pool[3][0:1, 0:8], lhsT=onesb[:, 0:1],
                              rhs=sbRt[:, k * 8:(k + 1) * 8],
                              start=(k == 0), stop=(k == NK - 1))
                t_.drain().then_inc(s_tsetup, 1)
                for b in range(NB):
                    t_.wait_ge(s_wsldP[b % 2], 16 * (b // 2 + 1))
                    if b >= 2:
                        t_.wait_ge(s_vwsum, b - 1)
                    for k in range(NK):
                        t_.matmul(pool[4 + (b % 2)][0:1, 0:128], lhsT=onesb[:, 0:1],
                                  rhs=wpool[:, (b % 2) * D + k * 128:(b % 2) * D + (k + 1) * 128],
                                  start=(k == 0), stop=(k == NK - 1))
                    t_.drain().then_inc(s_wsum, 1)

            for f, dst in ffns:
                t = f if f < NT else f - NT
                inst = inst_of(f)
                rb = 4 * inst
                if f == 0:
                    t_.wait_ge(s_vinit, 8)
                t_.wait_ge(s_vst, 2 * (inst + 1))
                for j in range(4):
                    src = bacc if j < 2 else bacc2
                    h = j % 2
                    t_.wait_ge(s_vrow, rb + j)
                    t_.matmul(ps_row[0:1, 0:512], lhsT=onesb[:, 0:1],
                              rhs=src[:, h * 512:(h + 1) * 512],
                              start=True, stop=True)
                    t_.drain().then_inc(s_trow, 1)
                t_.wait_ge(s_vrsh, f + 1)
                if f > 0:
                    t_.wait_ge(s_vevac, 32 * f)
                    if gate_tiles and f <= NT:
                        t_.wait_ge(s_sevac, 32 * min(f, NT))
                for h in range(2):
                    mm = t_.matmul(pool[h][:, 0:512], lhsT=onesf[0:1, :],
                                   rhs=rows[0:1, 2 * TT + h * 512:2 * TT + (h + 1) * 512],
                                   start=True, stop=True)
                t_.drain().then_inc(s_tbc1, 1)
                if f == 0:
                    setup_sums()
                    t_.wait_ge(s_vsetup, 2)
                    t_.wait_ge(s_vwsum, NB)
                t_.wait_ge(s_smbf, inst + 1)
                for c8 in range(8):
                    for k in range(NK):
                        t_.matmul(pool[4][:, 8 * c8:8 * c8 + 8],
                                  lhsT=xbf[:, k * TT + c8 * 128:k * TT + (c8 + 1) * 128],
                                  rhs=sbRt[:, k * 8:(k + 1) * 8],
                                  start=(k == 0), stop=False)
                    mm = t_.matmul(pool[4][:, 8 * c8:8 * c8 + 8],
                                   lhsT=mbf[0:1, c8 * 128:(c8 + 1) * 128],
                                   rhs=rsumn[0:1, :], start=False, stop=True)
                t_.drain().then_inc(s_trt, 1)
                for h in range(2):
                    for k in range(NK):
                        t_.matmul(pool[2 + h][:, 0:512],
                                  lhsT=sbAT[:, k * 128:(k + 1) * 128],
                                  rhs=xbf[:, k * TT + h * 512:k * TT + (h + 1) * 512],
                                  start=(k == 0), stop=False)
                    t_.matmul(pool[2 + h][:, 0:512], lhsT=asumn[0:1, :],
                              rhs=mbf[0:1, h * 512:(h + 1) * 512],
                              start=False, stop=True)
                    t_.drain().then_inc(s_tA, 1)
                t_.wait_ge(s_voh, f + 1)
                for c8 in range(8):
                    mm = t_.transpose(ps_tp[0:8, c8 * 128:(c8 + 1) * 128],
                                      in_=oh_sb[:, 8 * c8:8 * (c8 + 1)], identity=idbf[:])
                t_.drain().then_inc(s_ttp, 1)
                t_.wait_ge(s_soh, f + 1)
                for h in range(2):
                    mm = t_.matmul(pool[4 + h][:, 0:512], lhsT=exbf[:, :],
                                   rhs=ohT[0:8, h * 512:(h + 1) * 512],
                                   start=True, stop=True)
                t_.drain().then_inc(s_tmask, 1)
                for u in range(32):
                    b, h = u // 2, u % 2
                    bank = u % 6
                    if u >= 6:
                        t_.wait_ge(s_vevac, 32 * f + u - 5)
                    elif bank in (0, 1):
                        t_.wait_ge(s_sbc1, 2 * (f + 1))
                    elif bank in (2, 3):
                        t_.wait_ge(s_vhbf, 2 * (f + 1))
                    else:
                        t_.wait_ge(s_smask, 2 * (f + 1))
                    _idx = NK * f + b
                    if f >= NT:
                        t_.wait_ge(s_wbP4[_idx % 4], 16 * ((_idx - NK * NT) // 4 + 1))
                        wslot = _idx % 4
                    else:
                        t_.wait_ge(s_wbP[_idx % 2], 16 * (_idx // 2 + 1))
                        wslot = _idx % 2
                    for k in range(NK):
                        t_.matmul(pool[bank][:, 0:512],
                                  lhsT=wpool[:, wslot * D + k * 128:wslot * D + (k + 1) * 128],
                                  rhs=xbf[:, k * TT + h * 512:k * TT + (h + 1) * 512],
                                  start=(k == 0), stop=False)
                    t_.matmul(pool[bank][:, 0:512],
                              lhsT=wsumn[0:1, b * 128:(b + 1) * 128],
                              rhs=mbf[0:1, h * 512:(h + 1) * 512],
                              start=False, stop=False)
                    t_.matmul(pool[bank][:, 0:512],
                              lhsT=sbBwT[:, b * 128:(b + 1) * 128],
                              rhs=hbf[:, h * 512:(h + 1) * 512],
                              start=False, stop=True)
                    t_.drain().then_inc(s_tbase, 1)
                if f < NT and t in gate_tiles:
                    inst2 = 2 * t + 1
                    rb2 = 4 * inst2
                    t_.wait_ge(s_vst, 2 * (inst2 + 1))
                    for j in range(4):
                        src = bacc if j < 2 else bacc2
                        h = j % 2
                        t_.wait_ge(s_vrow, rb2 + j)
                        t_.matmul(ps_row[0:1, 0:512], lhsT=onesb[:, 0:1],
                                  rhs=src[:, h * 512:(h + 1) * 512],
                                  start=True, stop=True)
                        t_.drain().then_inc(s_trow, 1)
                    t_.wait_ge(s_sfin, inst2 + 1)
                    t_.wait_ge(s_vfin, inst2 + 1)
                    t_.wait_ge(s_vevac, 32 * (f + 1))
                    for j in range(4):
                        srcoff = (0 if j < 2 else TT) + (j % 2) * 512
                        mm = t_.matmul(pool[j][:, 0:512], lhsT=onesf[0:1, :],
                                       rhs=rows[0:1, srcoff:srcoff + 512],
                                       start=True, stop=True)
                    t_.drain().then_inc(s_tbc2, 1)
                    for c in range(32):
                        b, gate = c // 2, c % 2
                        cg = 32 * t + c
                        bank0, bank1 = (2 * c) % 6, (2 * c + 1) % 6
                        if c >= 3:
                            t_.wait_ge(s_sevac, cg - 2)
                        else:
                            t_.wait_ge(s_sbc2, 4 * (t + 1))
                        _ig = NK * t + b
                        t_.wait_ge((s_wgaP if gate == 0 else s_wgxP)[_ig % 2],
                                   16 * (_ig // 2 + 1))
                        woff = ((2 if gate == 0 else 4) + b % 2) * D
                        for h in range(2):
                            t_.wait_ge(s_vap, 32 * t + 16 * (h + 1))
                            for k in range(NK):
                                mm = t_.matmul(pool[(bank0, bank1)[h]][:, 0:512],
                                               lhsT=wpool[:, woff + k * 128:woff + (k + 1) * 128],
                                               rhs=xbf[:, k * TT + h * 512:k * TT + (h + 1) * 512],
                                               start=(k == 0), stop=(k == NK - 1))
                        t_.drain().then_inc(s_tgate, 1)

        @block.scalar
        def _(s):
            s.wait_ge(s_vinit, 8)
            s.activation(out=ksc[:, 0:NB], in_=bload[:, 3 * NB:4 * NB], func=AF.Exp,
                         bias=0.0, scale=-1.0).then_inc(s_sinit, 1)
            s.wait_ge(s_sinit, 1)
            s.activation(out=ksc[:, NB:2 * NB], in_=ksc[:, 0:NB], func=AF.Ln,
                         bias=1.0, scale=1.0).then_inc(s_sinit, 1)

            for f, dst in ffns:
                t = f if f < NT else f - NT
                inst = inst_of(f)
                if f >= NT and mode == "full":
                    if f == NT:
                        s.wait_ge(s_spill, 256 * NT)
                    for k in range(NK):
                        idx = NB * t + k
                        if t > 0:
                            s.wait_ge(s_vevac, 32 * (NT + t - 1) + 2 * (k + 1))
                        if idx >= 2:
                            s.wait_ge(s_gx, NB * NT + 2 * (idx - 1))
                        s.dma_start(out=x1b[:, k * TT:(k + 1) * TT],
                                    in_=x2p_d.ap()[t, k]).then_inc(s_x2rP[k % 2], 16)
                s.wait_ge(s_vrow, 4 * inst + 2)
                s.copy(out=mbf[0:1, :], in_=rows[0:1, 0:TT]).then_inc(s_smbf, 1)
                s.wait_ge(s_vfin, inst + 1)
                s.activation(out=rows[0:1, 2 * TT:3 * TT], in_=rows[0:1, TT:2 * TT],
                             func=AF.Ln, bias=epsrow[0:1, 0:1], scale=1.0)
                s.activation(out=rows[0:1, TT:2 * TT], in_=rows[0:1, 2 * TT:3 * TT],
                             func=AF.Exp, bias=0.0, scale=-0.5).then_inc(s_sfin, 1)
                if f == 0:
                    s.wait_ge(s_tsetup, 1)
                    s.copy(out=asumn[:], in_=pool[2][0:1, 0:128]).then_inc(s_vsetup, 1)
                    s.wait_ge(s_tsetup, 2)
                    s.copy(out=rsumn[:], in_=pool[3][0:1, 0:8]).then_inc(s_vsetup, 1)
                    for b in range(NB):
                        s.wait_ge(s_wsum, b + 1)
                        s.copy(out=wsumn[0:1, b * 128:(b + 1) * 128],
                               in_=pool[4 + (b % 2)][0:1, 0:128]).then_inc(s_vwsum, 1)
                for c8 in range(8):
                    s.wait_ge(s_vrx, 8 * f + c8 + 1)
                    s.activation(out=tmp_e[:, 8 * c8:8 * c8 + 8],
                                 in_=pool[4][:, 8 * c8:8 * c8 + 8],
                                 func=AF.Exp, bias=rmaxc[:, c8:c8 + 1],
                                 scale=1.0).then_inc(s_srx, 1)
                s.wait_ge(s_tbc1, f + 1)
                s.copy(out=rsBh[:, 0:512], in_=pool[0][:, 0:512]).then_inc(s_sbc1, 1)
                s.copy(out=rsBh[:, 512:TT], in_=pool[1][:, 0:512]).then_inc(s_sbc1, 1)
                s.wait_ge(s_ttp, f + 1)
                s.copy(out=ohT[:], in_=ps_tp[:]).then_inc(s_soh, 1)
                s.wait_ge(s_tmask, f + 1)
                s.copy(out=maskbf[:, 0:512], in_=pool[4][:, 0:512]).then_inc(s_smask, 1)
                s.copy(out=maskbf[:, 512:TT], in_=pool[5][:, 0:512]).then_inc(s_smask, 1)
                if f < NT and t in gate_tiles:
                    inst2 = 2 * t + 1
                    s.sem_inc(s_smbf, 1)
                    s.wait_ge(s_vfin, inst2 + 1)
                    s.activation(out=rows[0:1, 2 * TT:3 * TT], in_=rows[0:1, TT:2 * TT],
                                 func=AF.Ln, bias=epsrow[0:1, 0:1], scale=1.0)
                    s.activation(out=rows[0:1, TT:2 * TT], in_=rows[0:1, 2 * TT:3 * TT],
                                 func=AF.Exp, bias=0.0, scale=-0.5).then_inc(s_sfin, 1)
                    s.wait_ge(s_tbc2, t + 1)
                    s.copy(out=mB[:, 0:512], in_=pool[0][:, 0:512]).then_inc(s_sbc2, 1)
                    s.copy(out=mB[:, 512:TT], in_=pool[1][:, 0:512]).then_inc(s_sbc2, 1)
                    s.copy(out=rsB[:, 0:512], in_=pool[2][:, 0:512]).then_inc(s_sbc2, 1)
                    s.copy(out=rsB[:, 512:TT], in_=pool[3][:, 0:512]).then_inc(s_sbc2, 1)
                    for b in range(NB):
                        bb = NB * t + b
                        if bb > 0:
                            s.wait_ge(s_vy, 2 * bb)
                            s.wait_ge(s_gx, bb)
                        s.wait_ge(s_tgate, 32 * t + 2 * b + 1)
                        for h in range(2):
                            aa = s.activation(out=S[0][:, h * 512:(h + 1) * 512],
                                              in_=pool[(4 * b + h) % 6][:, 0:512],
                                              func=AF.Exp, bias=negba[:, b:b + 1], scale=-1.0)
                        aa.then_inc(s_sevac, 1)
                        s.activation(out=S[1][:, 0:TT], in_=S[0][:, 0:TT],
                                     func=AF.Ln, bias=1.0, scale=1.0)
                        s.activation(out=S[0][:, 0:TT], in_=S[1][:, 0:TT],
                                     func=AF.Exp, bias=0.0, scale=-1.0).then_inc(s_sr, 1)
                        s.wait_ge(s_tgate, 32 * t + 2 * b + 2)
                        for h in range(2):
                            aa = s.activation(out=S[3][:, h * 512:(h + 1) * 512],
                                              in_=pool[(4 * b + 2 + h) % 6][:, 0:512],
                                              func=AF.Exp, bias=negbx[:, b:b + 1], scale=-1.0)
                        aa.then_inc(s_sevac, 1)
                        s.activation(out=S[4][:, 0:TT], in_=S[3][:, 0:TT],
                                     func=AF.Ln, bias=1.0, scale=1.0)
                        s.activation(out=S[3][:, 0:TT], in_=S[4][:, 0:TT],
                                     func=AF.Exp, bias=0.0, scale=-1.0).then_inc(s_si, 1)
                        s.wait_ge(s_vla, bb + 1)
                        s.activation(out=S[2][:, 0:TT], in_=S[1][:, 0:TT],
                                     func=AF.Exp, bias=0.0, scale=1.0).then_inc(s_sa, 1)
                        s.activation(out=S[0][:, 0:TT], in_=S[1][:, 0:TT],
                                     func=AF.Exp, bias=0.0, scale=2.0)
                        s.activation(out=S[5][:, 0:TT], in_=S[0][:, 0:TT],
                                     func=AF.Ln, bias=1.0, scale=-1.0)
                        s.activation(out=S[0][:, 0:TT], in_=S[5][:, 0:TT],
                                     func=AF.Exp, bias=0.0, scale=0.5).then_inc(s_sa, 1)
                        s.wait_ge(s_vy, 2 * (bb + 1))
                        s.copy(out=ysave[:, b:b + 1], in_=S[4][:, TT - 1:TT]).then_inc(s_ysv, 1)
                        s.copy(out=csave[:, b:b + 1], in_=S[3][:, TT - 1:TT]).then_inc(s_ysv, 1)
                        if bb >= 2:
                            s.wait_ge(s_pspill, 16 * (bb - 1))
                        s.activation(out=pstage[:, (b % 2) * TT:(b % 2 + 1) * TT],
                                     in_=S[3][:, 0:TT], func=AF.Exp,
                                     bias=0.0, scale=1.0).then_inc(s_sp9, 1)

        @block.vector
        def _(v):
            v.wait_ge(s_giota, 1)
            v.wait_ge(s_ld, 64)
            if dbgf_d is not None:
                v.memset(topguard[:, 0:1024], 123.25)
            v.memset(onesb[:], -1.0).then_inc(s_vinit, 1)
            v.memset(onesf[:], 1.0).then_inc(s_vinit, 1)
            v.memset(epsrow[:], EPS).then_inc(s_vinit, 1)
            v.tensor_scalar(out=idbf[:], in0=id_i[:], scalar1=0, scalar2=None,
                            op0=ALU.is_equal).then_inc(s_vinit, 1)
            v.tensor_scalar(out=exbf[:], in0=ex_i[:], scalar1=0, scalar2=2.0,
                            op0=ALU.is_equal, op1=ALU.mult).then_inc(s_vinit, 1)
            v.tensor_scalar(out=negba[:], in0=bload[:, NB:2 * NB], scalar1=-1.0,
                            scalar2=None, op0=ALU.mult).then_inc(s_vinit, 1)
            v.tensor_scalar(out=negbx[:], in0=bload[:, 2 * NB:3 * NB], scalar1=-1.0,
                            scalar2=None, op0=ALU.mult).then_inc(s_vinit, 1)
            v.tensor_scalar(out=hbrow[:], in0=bload[:, 0:NB], scalar1=0.5,
                            scalar2=None, op0=ALU.mult).then_inc(s_vinit, 1)
            v.wait_ge(s_sinit, 2)
            v.tensor_scalar(out=k1h[:], in0=ksc[:, NB:2 * NB], scalar1=-8.0,
                            scalar2=None, op0=ALU.mult)

            def stats_partials():
                for k in range(NK):
                    xk = xbf[:, k * TT:(k + 1) * TT]
                    v.tensor_tensor(out=bsq[:, 0:TT], in0=xk, in1=xk, op=ALU.mult)
                    if k == 0:
                        v.tensor_copy(bacc[:, 0:TT], xk)
                        v.tensor_copy(bacc2[:, 0:TT], bsq[:, 0:TT])
                    else:
                        acci = v.tensor_tensor(out=bacc[:, 0:TT], in0=bacc[:, 0:TT],
                                        in1=xk, op=ALU.add)
                        last = v.tensor_tensor(out=bacc2[:, 0:TT], in0=bacc2[:, 0:TT],
                                               in1=bsq[:, 0:TT], op=ALU.add)
                        if k == NK - 1:
                            acci.then_inc(s_vst, 1)
                return last

            def ln_rows_and_finish(inst, is_ffn, f):
                rb = 4 * inst
                for j in range(4):
                    h = j % 2
                    if j == 0 and inst > 0:
                        v.wait_ge(s_smbf, inst)
                    v.wait_ge(s_trow, rb + j + 1)
                    dstoff = (0 if j < 2 else TT) + h * 512
                    v.tensor_scalar(out=rows[0:1, dstoff:dstoff + 512],
                                    in0=ps_row[0:1, 0:512], scalar1=-INV_D,
                                    scalar2=None, op0=ALU.mult).then_inc(s_vrow, 1)
                v.tensor_tensor(out=rows[0:1, 2 * TT:3 * TT], in0=rows[0:1, 0:TT],
                                in1=rows[0:1, 0:TT], op=ALU.mult)
                v.tensor_tensor(out=rows[0:1, TT:2 * TT], in0=rows[0:1, TT:2 * TT],
                                in1=rows[0:1, 2 * TT:3 * TT],
                                op=ALU.subtract).then_inc(s_vfin, 1)
                if is_ffn:
                    v.wait_ge(s_sfin, inst + 1)
                    v.tensor_scalar(out=rows[0:1, 2 * TT:3 * TT],
                                    in0=rows[0:1, TT:2 * TT], scalar1=0.5,
                                    scalar2=None, op0=ALU.mult).then_inc(s_vrsh, 1)

            for f, dst in ffns:
                t = f if f < NT else f - NT
                inst = inst_of(f)
                if f < NT:
                    v.wait_ge(s_xbf, 16 * (t + 1))
                    if inst > 0:
                        v.wait_ge(s_trow, 4 * inst)
                    stats_partials().then_inc(s_vst, 1)
                else:
                    if f == NT:
                        v.wait_ge(s_hfin, 1)
                        v.wait_ge(s_tgate, 32 * NT)
                    v.wait_ge(s_trow, 4 * inst)
                    for k in range(NK):
                        v.wait_ge(s_x2rP[k % 2], 16 * (t * 8 + k // 2 + 1))
                        v.wait_ge(s_pldP[k % 2], 16 * ((NB * t + k) // 2 + 1))
                        v.scalar_tensor_tensor(out=x1b[:, k * TT:(k + 1) * TT],
                                               in0=pstage[:, (k % 2) * TT:(k % 2 + 1) * TT],
                                               scalar=hfin[:, k:k + 1],
                                               in1=x1b[:, k * TT:(k + 1) * TT],
                                               op0=ALU.mult, op1=ALU.add).then_inc(s_gx, 1)
                        v.tensor_scalar(out=xbf[:, k * TT:(k + 1) * TT],
                                        in0=x1b[:, k * TT:(k + 1) * TT],
                                        scalar1=1.0, scalar2=None,
                                        op0=ALU.mult).then_inc(s_gx, 1)
                    stats_partials().then_inc(s_vst, 1)
                ln_rows_and_finish(inst, True, f)
                v.wait_ge(s_trt, f + 1)
                for c8 in range(8):
                    v.tensor_reduce(out=rmaxc[:, c8:c8 + 1], in_=pool[4][:, 8 * c8:8 * c8 + 8],
                                    axis=AX.X, op=ALU.max, negate=True).then_inc(s_vrx, 1)
                v.wait_ge(s_srx, 8 * (f + 1))
                v.tensor_scalar(out=oh_sb[:, 0:64], in0=tmp_e[:, 0:64],
                                scalar1=1.0, scalar2=None,
                                op0=ALU.is_ge).then_inc(s_voh, 1)
                v.wait_ge(s_tA, 2 * f + 2)
                v.wait_ge(s_smask, 2 * (f + 1))
                for h in range(2):
                    v.tensor_tensor(out=hbf[:, h * 512:(h + 1) * 512],
                                    in0=pool[2 + h][:, 0:512],
                                    in1=maskbf[:, h * 512:(h + 1) * 512],
                                    op=ALU.mult).then_inc(s_vhbf, 1)
                v.wait_ge(s_sbc1, 2 * (f + 1))
                if f < NT:
                    v.wait_ge(s_x0, 16 * (f + 1))
                for u in range(32):
                    b, h = u // 2, u % 2
                    bank = u % 6
                    v.wait_ge(s_tbase, 32 * f + u + 1)
                    if dst == "out" and (NK * t + b) >= 2 and h == 0:
                        v.wait_ge(s_out, 16 * (NK * t + b))
                    te = tmp_e[:, 0:512]
                    v.tensor_tensor(out=te, in0=pool[bank][:, 0:512],
                                    in1=rsBh[:, h * 512:(h + 1) * 512], op=ALU.mult)
                    x1sl = x1b[:, b * TT + h * 512:b * TT + (h + 1) * 512]
                    if dst == "x1":
                        v.scalar_tensor_tensor(out=x1sl, in0=te, scalar=hbrow[:, b:b + 1],
                                               in1=x1sl, op0=ALU.add,
                                               op1=ALU.add).then_inc(s_vevac, 1)
                        if t in gate_tiles:
                            v.tensor_tensor(out=bsq[:, h * 512:(h + 1) * 512], in0=x1sl,
                                            in1=x1sl, op=ALU.mult)
                            if b == 0:
                                v.tensor_scalar(out=bacc[:, h * 512:(h + 1) * 512], in0=x1sl,
                                                scalar1=1.0, scalar2=None, op0=ALU.mult)
                                v.tensor_copy(bacc2[:, h * 512:(h + 1) * 512],
                                              bsq[:, h * 512:(h + 1) * 512])
                            else:
                                acci = v.tensor_tensor(out=bacc[:, h * 512:(h + 1) * 512],
                                                in0=bacc[:, h * 512:(h + 1) * 512],
                                                in1=x1sl, op=ALU.add)
                                acc2i = v.tensor_tensor(out=bacc2[:, h * 512:(h + 1) * 512],
                                                in0=bacc2[:, h * 512:(h + 1) * 512],
                                                in1=bsq[:, h * 512:(h + 1) * 512], op=ALU.add)
                                if u == 31:
                                    acci.then_inc(s_vst, 1)
                                    acc2i.then_inc(s_vst, 1)
                    else:
                        stg = S[b % 2][:, h * 512:(h + 1) * 512]
                        v.scalar_tensor_tensor(out=stg, in0=te, scalar=hbrow[:, b:b + 1],
                                               in1=x1sl, op0=ALU.add,
                                               op1=ALU.add).then_inc(s_vevac, 1)
                if f < NT and t in gate_tiles:
                    inst2 = 2 * t + 1
                    ln_rows_and_finish(inst2, False, f)
                    v.wait_ge(s_sbc2, 4 * (t + 1))
                    for h in range(2):
                        for b in range(NB):
                            x1sl = x1b[:, b * TT + h * 512:b * TT + (h + 1) * 512]
                            te = tmp_e[:, 0:512]
                            v.tensor_tensor(out=te, in0=x1sl,
                                            in1=mB[:, h * 512:(h + 1) * 512], op=ALU.subtract)
                            v.tensor_tensor(out=xbf[:, b * TT + h * 512:b * TT + (h + 1) * 512],
                                            in0=te, in1=rsB[:, h * 512:(h + 1) * 512],
                                            op=ALU.mult).then_inc(s_vap, 1)
                    for b in range(NB):
                        bb = NB * t + b
                        v.wait_ge(s_sr, bb + 1)
                        v.tensor_scalar(out=S[1][:, 0:TT], in0=S[0][:, 0:TT],
                                        scalar1=k1h[:, b:b + 1], scalar2=LACLAMP,
                                        op0=ALU.mult, op1=ALU.min).then_inc(s_vla, 1)
                        v.wait_ge(s_sa, 2 * (bb + 1))
                        v.wait_ge(s_si, bb + 1)
                        x1sl = x1b[:, b * TT:(b + 1) * TT]
                        v.tensor_tensor(out=S[4][:, 0:TT], in0=x1sl, in1=mB[:, 0:TT],
                                        op=ALU.subtract)
                        v.tensor_tensor(out=S[5][:, 0:TT], in0=S[4][:, 0:TT],
                                        in1=rsB[:, 0:TT], op=ALU.mult)
                        v.tensor_tensor(out=S[4][:, 0:TT], in0=S[3][:, 0:TT],
                                        in1=S[5][:, 0:TT], op=ALU.mult)
                        v.tensor_tensor(out=S[5][:, 0:TT], in0=S[4][:, 0:TT],
                                        in1=S[0][:, 0:TT], op=ALU.mult)
                        if t > 0:
                            v.wait_ge(s_ysv, 2 * (NB * (t - 1) + b + 1))
                        init_y = 0.0 if t == 0 else ysave[:, b:b + 1]
                        v.tensor_tensor_scan(out=S[4][:, 0:TT], data0=S[2][:, 0:TT],
                                             data1=S[5][:, 0:TT], initial=init_y,
                                             op0=ALU.mult, op1=ALU.add).then_inc(s_vy, 1)
                        init_c = 0.0 if t == 0 else csave[:, b:b + 1]
                        v.tensor_tensor_scan(out=S[3][:, 0:TT], data0=S[1][:, 0:TT],
                                             data1=S[1][:, 0:TT], initial=init_c,
                                             op0=ALU.add, op1=ALU.bypass).then_inc(s_vy, 1)

    return nc


def kernel(L0_attn_x, L0_attn_Wa, L0_attn_Wx, L0_attn_ba, L0_attn_bx,
           L0_attn_Lam, L0_ffn_W_base, L0_ffn_b_base, L0_ffn_router_W,
           L0_ffn_A, L0_ffn_Bw):
    mode = _MODE
    B = 4

    xT = np.ascontiguousarray(np.asarray(L0_attn_x, np.float32).transpose(0, 2, 1))

    def lhsT_blocks(W):
        return np.ascontiguousarray(
            np.asarray(W, np.float32).T.reshape(D, NB, 128).transpose(1, 0, 2))

    wb = lhsT_blocks(L0_ffn_W_base)
    wa = lhsT_blocks(L0_attn_Wa)
    wx = lhsT_blocks(L0_attn_Wx)
    rt = np.ascontiguousarray(np.asarray(L0_ffn_router_W, np.float32).T)
    at = np.ascontiguousarray(np.asarray(L0_ffn_A, np.float32).reshape(128, D).T)
    bwt = np.ascontiguousarray(
        np.asarray(L0_ffn_Bw, np.float32).transpose(0, 2, 1).reshape(128, D))

    def col16(vec):
        return np.ascontiguousarray(np.asarray(vec, np.float32).reshape(NB, 128).T)

    bvec = np.ascontiguousarray(np.concatenate(
        [col16(L0_ffn_b_base), col16(L0_attn_ba),
         col16(L0_attn_bx), col16(L0_attn_Lam)], axis=1))

    nonce = np.full((1, 8), int(time.time_ns() // 1000) % (2**30) + 1, np.int32)

    nc = build_nc(mode)
    in_maps = []
    for c in range(8):
        b, half = c // 2, c % 2
        in_maps.append(dict(
            x=np.ascontiguousarray(xT[b, :, half * TPC:(half + 1) * TPC]),
            wb=wb, wa=wa, wx=wx, rt=rt, at=at, bwt=bwt, bvec=bvec,
            hmask=np.full((128, 1), float(half), np.float32),
            slot=np.array([[half]], np.int32),
            nonce=nonce,
        ))
    res = run_bass_kernel_spmd(nc, in_maps, core_ids=list(range(8)))

    outT = np.empty((B, D, 2 * TPC), np.float32)
    for c in range(8):
        b, half = c // 2, c % 2
        o = res.results[c]["out"]
        o = np.asarray(o).transpose(1, 2, 0, 3).reshape(D, TPC)
        outT[b, :, half * TPC:(half + 1) * TPC] = o
    return np.ascontiguousarray(outT.transpose(0, 2, 1))


# version 29
# speedup vs baseline: 1.4120x; 1.0027x over previous
import time
import numpy as np
import ml_dtypes
from contextlib import ExitStack

import concourse.bass as bass
import concourse.mybir as mybir
from concourse.bass_utils import run_bass_kernel_spmd

F32 = mybir.dt.float32
BF16 = mybir.dt.bfloat16
I32 = mybir.dt.int32
AF = mybir.ActivationFunctionType
ALU = mybir.AluOpType
AX = mybir.AxisListType

D = 2048
TPC = 4096
TT = 1024
NT = TPC // TT
NB = 16
NK = 16
EPS = 1e-5
LACLAMP = -2e-6
INV_D = 1.0 / D

_MODE = "full"


def build_nc(mode):
    nc = bass.Bass("TRN2", target_bir_lowering=False)

    x_in = nc.dram_tensor("x", [D, TPC], F32, kind="ExternalInput").ap()
    wb_in = nc.dram_tensor("wb", [NB, D, 128], F32, kind="ExternalInput").ap()
    wa_in = nc.dram_tensor("wa", [NB, D, 128], F32, kind="ExternalInput").ap()
    wx_in = nc.dram_tensor("wx", [NB, D, 128], F32, kind="ExternalInput").ap()
    rt_in = nc.dram_tensor("rt", [D, 8], F32, kind="ExternalInput").ap()
    at_in = nc.dram_tensor("at", [D, 128], F32, kind="ExternalInput").ap()
    bwt_in = nc.dram_tensor("bwt", [128, D], F32, kind="ExternalInput").ap()
    bvec_in = nc.dram_tensor("bvec", [128, 4 * NB], F32, kind="ExternalInput").ap()
    hmask_in = nc.dram_tensor("hmask", [128, 1], F32, kind="ExternalInput").ap()
    slot_in = nc.dram_tensor("slot", [1, 1], I32, kind="ExternalInput").ap()
    nonce_in = nc.dram_tensor("nonce", [1, 8], I32, kind="ExternalInput").ap()

    out_d = nc.dram_tensor("out", [NT, NB, 128, TT], F32, kind="ExternalOutput").ap()
    dbg_d = None
    if mode == "x2p":
        dbg_d = nc.dram_tensor("dbg", [NT, 8, 128, TT], F32, kind="ExternalOutput").ap()
    dbgf_d = None
    if mode == "ffn1":
        dbgf_d = nc.dram_tensor("dbgf", [20, 2048], F32, kind="ExternalOutput").ap()

    wbb = nc.dram_tensor("wbb", [NB, 128, D], BF16, kind="Internal")
    wab = nc.dram_tensor("wab", [NB, 128, D], BF16, kind="Internal")
    wxb = nc.dram_tensor("wxb", [NB, 128, D], BF16, kind="Internal")
    x2p_d = nc.dram_tensor("x2pd", [NT, NB, 128, TT], F32, kind="Internal")
    p_d = nc.dram_tensor("pdd", [NT, NB, 128, TT], BF16, kind="Internal")
    ex_d = nc.dram_tensor("exd", [2, 128, NB], F32, kind="Internal", addr_space="Shared")
    fl_d = nc.dram_tensor("fld", [2, 1, 8], I32, kind="Internal", addr_space="Shared")

    ctx = ExitStack()
    with ctx:
        def sbt(name, shape, dt=F32):
            return ctx.enter_context(nc.sbuf_tensor(name, shape, dt))

        def sem(name):
            return ctx.enter_context(nc.semaphore(name))

        x1b = sbt("x1b", [128, NB * TT])
        xbf = sbt("xbf", [128, NB * TT], BF16)
        wpool = sbt("wpool", [128, 6 * D], BF16)
        sbAT = sbt("sbAT", [128, D], BF16)
        sbBwT = sbt("sbBwT", [128, D], BF16)
        sbRt = sbt("sbRt", [128, NK * 8], BF16)
        S = [sbt(f"gs{i}", [128, TT]) for i in range(6)]
        tmp_e = sbt("tmp_e", [128, 512])
        bacc = sbt("bacc", [128, TT], BF16)
        bacc2 = sbt("bacc2", [128, TT], BF16)
        bsq = sbt("bsq", [128, TT], BF16)
        mB = sbt("mB", [128, TT])
        rsB = sbt("rsB", [128, TT])
        rsBh = sbt("rsBh", [128, TT])
        rows = sbt("rows", [1, 3 * TT])
        mbf = sbt("mbf", [1, TT], BF16)
        maskbf = sbt("maskbf", [128, TT], BF16)
        hbf = sbt("hbf", [128, TT], BF16)
        oh_sb = sbt("oh_sb", [128, 64], BF16)
        ohT = sbt("ohT", [8, TT], BF16)
        pstage = sbt("pstage", [128, 2 * TT], BF16)
        ysave = sbt("ysave", [128, NB])
        csave = sbt("csave", [128, NB])
        bload = sbt("bload", [128, 4 * NB])
        k1h = sbt("k1h", [128, NB])
        negba = sbt("negba", [128, NB])
        negbx = sbt("negbx", [128, NB])
        hbrow = sbt("hbrow", [128, NB])
        ksc = sbt("ksc", [128, 2 * NB])
        wsumn = sbt("wsumn", [1, D], BF16)
        asumn = sbt("asumn", [1, 128], BF16)
        rsumn = sbt("rsumn", [1, 8], BF16)
        onesb = sbt("onesb", [128, 1], BF16)
        onesf = sbt("onesf", [1, 128])
        id_i = sbt("id_i", [128, 128], I32)
        idbf = sbt("idbf", [128, 128], BF16)
        ex_i = sbt("ex_i", [8, 128], I32)
        exbf = sbt("exbf", [8, 128], BF16)
        hraw = sbt("hraw", [128, NB])
        hfin = sbt("hfin", [128, NB])
        hmk = sbt("hmk", [128, 1])
        sbslot = sbt("sbslot", [1, 1], I32)
        sbnon = sbt("sbnon", [1, 8], I32)
        sbfl = sbt("sbfl", [1, 16], I32)
        epsrow = sbt("epsrow", [1, 1])
        rmaxc = sbt("rmaxc", [128, 8])
        topguard = sbt("topguard", [128, 1024])

        pool = [ctx.enter_context(nc.psum_tensor(f"pp{i}", [128, 512], F32)) for i in range(6)]
        ps_row = ctx.enter_context(nc.psum_tensor("ps_row", [1, 512], F32))
        ps_tp = ctx.enter_context(nc.psum_tensor("ps_tp", [8, TT], BF16))

        s_ld = sem("s_ld");  s_cast = sem("s_cast");  s_giota = sem("s_giota")
        s_vinit = sem("s_vinit");  s_sinit = sem("s_sinit")
        s_tsetup = sem("s_tsetup");  s_vsetup = sem("s_vsetup")
        s_x0 = sem("s_x0");  s_xbf = sem("s_xbf")
        s_wb = sem("s_wb");  s_wg = sem("s_wg")
        s_vst = sem("s_vst");  s_trow = sem("s_trow");  s_vrow = sem("s_vrow")
        s_vfin = sem("s_vfin");  s_sfin = sem("s_sfin");  s_vrsh = sem("s_vrsh")
        s_tbc1 = sem("s_tbc1");  s_sbc1 = sem("s_sbc1")
        s_tbc2 = sem("s_tbc2");  s_sbc2 = sem("s_sbc2")
        s_trt = sem("s_trt");  s_voh = sem("s_voh");  s_ttp = sem("s_ttp")
        s_soh = sem("s_soh");  s_tmask = sem("s_tmask");  s_smask = sem("s_smask")
        s_tA = sem("s_tA");  s_vhbf = sem("s_vhbf")
        s_tbase = sem("s_tbase");  s_vevac = sem("s_vevac")
        s_vap = sem("s_vap");  s_tgate = sem("s_tgate");  s_sevac = sem("s_sevac")
        s_sr = sem("s_sr");  s_si = sem("s_si");  s_vla = sem("s_vla")
        s_sa = sem("s_sa");  s_vy = sem("s_vy");  s_sp9 = sem("s_sp9")
        s_gx = sem("s_gx");  s_spill = sem("s_spill");  s_pspill = sem("s_pspill")
        s_pld = sem("s_pld");  s_exch = sem("s_exch");  s_poll = sem("s_poll")
        s_hfin = sem("s_hfin");  s_out = sem("s_out")
        s_wsum = sem("s_wsum");  s_vwsum = sem("s_vwsum");  s_ysv = sem("s_ysv")
        s_vrx = sem("s_vrx");  s_srx = sem("s_srx");  s_wsld = sem("s_wsld")
        s_smbf = sem("s_smbf");  s_ydbg = sem("s_ydbg")
        s_cwb = sem("s_cwb");  s_cmisc = sem("s_cmisc")
        s_cwa = sem("s_cwa");  s_cwx = sem("s_cwx")
        s_wsldP = [sem("s_wsldA"), sem("s_wsldB")]
        s_wbP = [sem("s_wbA"), sem("s_wbB")]
        s_wgaP = [sem("s_wgaA"), sem("s_wgaB")]
        s_wgxP = [sem("s_wgxA"), sem("s_wgxB")]
        s_pldP = [sem("s_pldA"), sem("s_pldB")]
        s_x2rP = [sem("s_x2rA"), sem("s_x2rB")]

        block = ctx.enter_context(nc.Block())

        if mode == "ffn1":
            ffns = [(f, "out") for f in range(NT)]
            gate_tiles = []
        elif mode == "x2p":
            ffns = [(f, "x1") for f in range(NT)]
            gate_tiles = list(range(NT))
        else:
            ffns = [(f, "x1") for f in range(NT)] + [(f, "out") for f in range(NT, 2 * NT)]
            gate_tiles = list(range(NT))

        def inst_of(f):
            if mode == "ffn1":
                return f
            if f < NT:
                return 2 * f
            return 2 * NT + (f - NT)

        x2p_spill = out_d if mode == "x2p" else x2p_d.ap()

        @block.sync
        def _(y):
            y.dma_start(out=bload[:], in_=bvec_in[:]).then_inc(s_ld, 16)
            y.dma_start(out=hmk[:], in_=hmask_in[:]).then_inc(s_ld, 16)
            y.dma_start(out=sbslot[:], in_=slot_in[:]).then_inc(s_ld, 16)
            y.dma_start(out=sbnon[:], in_=nonce_in[:]).then_inc(s_ld, 16)
            if dbgf_d is not None:
                y.wait_ge(s_vsetup, 1)
                y.dma_start(out=dbgf_d[16:17, 0:64],
                            in_=asumn[0:1, :].bitcast(F32)).then_inc(s_ydbg, 16)
                y.wait_ge(s_vsetup, 2)
                y.dma_start(out=dbgf_d[16:17, 64:68],
                            in_=rsumn[0:1, :].bitcast(F32)).then_inc(s_ydbg, 16)
            y.wait_ge(s_cwb, 256)
            for b in range(NB):
                if b >= 2:
                    y.wait_ge(s_wsum, b - 1)
                y.dma_start(out=wpool[:, (b % 2) * D:(b % 2 + 1) * D],
                            in_=wbb.ap()[b]).then_inc(s_wsldP[b % 2], 16)

            for f, dst in ffns:
                t = f if f < NT else f - NT
                if f < NT:
                    if t > 0:
                        if gate_tiles:
                            y.wait_ge(s_spill, 256 * t)
                        else:
                            y.wait_ge(s_vevac, 32 * t)
                    y.dma_start(
                        out=x1b[:].rearrange("p (k t) -> p k t", k=NK),
                        in_=x_in[:, t * TT:(t + 1) * TT].rearrange("(k p) t -> p k t", p=128),
                    ).then_inc(s_x0, 16)
                for b in range(NB):
                    idx = NK * f + b
                    if idx >= 2:
                        y.wait_ge(s_tbase, 2 * (idx - 1))
                    elif idx == 0:
                        y.wait_ge(s_wsum, NB - 1)
                    else:
                        y.wait_ge(s_wsum, NB)
                    y.dma_start(
                        out=wpool[:, (idx % 2) * D:(idx % 2 + 1) * D],
                        in_=wbb.ap()[b],
                    ).then_inc(s_wbP[idx % 2], 16)
                    if dst == "out" and b >= 2:
                        bo = b - 2
                        y.wait_ge(s_vevac, 32 * f + 2 * (bo + 1))
                        y.dma_start(out=out_d[t, bo], in_=S[bo % 2][:, 0:TT]).then_inc(s_out, 16)
                if dst == "out":
                    for bo in (NB - 2, NB - 1):
                        y.wait_ge(s_vevac, 32 * f + 2 * (bo + 1))
                        y.dma_start(out=out_d[t, bo], in_=S[bo % 2][:, 0:TT]).then_inc(s_out, 16)
                if f < NT and t in gate_tiles:
                    for b in range(NB):
                        if b >= 2:
                            y.wait_ge(s_tgate, 32 * t + 2 * (b - 2) + 2)
                        elif t > 0:
                            y.wait_ge(s_tgate, 32 * (t - 1) + 2 * (NB + b - 2) + 2)
                        if t == 0 and b == 0:
                            y.wait_ge(s_cwa, 256)
                            y.wait_ge(s_cwx, 256)
                        y.dma_start(
                            out=wpool[:, (2 + b % 2) * D:(2 + b % 2 + 1) * D],
                            in_=wab.ap()[b],
                        ).then_inc(s_wgaP[b % 2], 16)
                        y.dma_start(
                            out=wpool[:, (4 + b % 2) * D:(4 + b % 2 + 1) * D],
                            in_=wxb.ap()[b],
                        ).then_inc(s_wgxP[b % 2], 16)
            if mode == "x2p":
                y.wait_ge(s_spill, 256 * NT)
                y.wait_ge(s_pspill, 256 * NT)
            else:
                y.wait_ge(s_out, 16 * NB * NT)

        @block.gpsimd
        def _(g):
            g.iota(id_i[:], pattern=[[1, 128]], base=0, channel_multiplier=-1)
            g.iota(ex_i[:], pattern=[[1, 8], [0, 16]], base=0,
                   channel_multiplier=-1).then_inc(s_giota, 1)
            g.dma_start(
                out=xbf[:].rearrange("p (k t) -> p k t", k=NK),
                in_=x_in[:, 0:TT].rearrange("(k p) t -> p k t", p=128),
            ).then_inc(s_xbf, 16)
            for b in range(NB):
                g.dma_start(out=wbb.ap()[b].rearrange("p (k o) -> p k o", k=NK),
                            in_=wb_in[b].rearrange("(k p) o -> p k o", p=128)).then_inc(s_cwb, 16)
            g.dma_start(out=sbAT[:].rearrange("p (k e) -> p k e", k=NK),
                        in_=at_in.rearrange("(k p) e -> p k e", p=128)).then_inc(s_cmisc, 16)
            g.dma_start(out=sbBwT[:], in_=bwt_in[:]).then_inc(s_cmisc, 16)
            g.dma_start(out=sbRt[:].rearrange("p (k e) -> p k e", k=NK),
                        in_=rt_in.rearrange("(k p) e -> p k e", p=128)).then_inc(s_cmisc, 16)
            for b in range(NB):
                g.dma_start(out=wab.ap()[b].rearrange("p (k o) -> p k o", k=NK),
                            in_=wa_in[b].rearrange("(k p) o -> p k o", p=128)).then_inc(s_cwa, 16)
            for b in range(NB):
                g.dma_start(out=wxb.ap()[b].rearrange("p (k o) -> p k o", k=NK),
                            in_=wx_in[b].rearrange("(k p) o -> p k o", p=128)).then_inc(s_cwx, 16)

            if dbgf_d is not None:
                g.wait_ge(s_vwsum, NB)
                g.dma_start(out=dbgf_d[0:1, :], in_=wsumn[0:1, :]).then_inc(s_poll, 16)
                g.wait_ge(s_vsetup, 2)
                g.dma_start(out=dbgf_d[1:2, 0:128], in_=asumn[0:1, :]).then_inc(s_poll, 16)
                g.dma_start(out=dbgf_d[1:2, 128:136], in_=rsumn[0:1, :]).then_inc(s_poll, 16)
                g.wait_ge(s_vinit, 8)
                g.dma_start(out=dbgf_d[2:3, :].rearrange("a (p c) -> (a p) c", p=128),
                            in_=hbrow[:, 0:NB]).then_inc(s_poll, 16)
                g.dma_start(out=dbgf_d[3:4, 0:1024],
                            in_=wsumn[0:1, :].bitcast(F32)).then_inc(s_poll, 16)
                g.dma_start(out=dbgf_d[3:4, 1024:1088],
                            in_=asumn[0:1, :].bitcast(F32)).then_inc(s_poll, 16)
                g.dma_start(out=dbgf_d[3:4, 1088:1092],
                            in_=rsumn[0:1, :].bitcast(F32)).then_inc(s_poll, 16)
            for t in range(NT):
                if t > 0 and not gate_tiles:
                    g.wait_ge(s_tbase, 32 * t)
                    g.dma_start(
                        out=xbf[:].rearrange("p (k t) -> p k t", k=NK),
                        in_=x_in[:, t * TT:(t + 1) * TT].rearrange("(k p) t -> p k t", p=128),
                    ).then_inc(s_xbf, 16)
                if dbgf_d is not None:
                    g.wait_ge(s_sfin, t + 1)
                    g.dma_start(out=dbgf_d[4 + 2 * t:5 + 2 * t, 0:TT],
                                in_=rows[0:1, 0:TT]).then_inc(s_poll, 16)
                    g.dma_start(out=dbgf_d[5 + 2 * t:6 + 2 * t, 0:TT],
                                in_=rows[0:1, TT:2 * TT]).then_inc(s_poll, 16)
                if t not in gate_tiles:
                    continue
                for b in range(NB):
                    g.wait_ge(s_vy, 2 * (NB * t + b) + 1)
                    if dbg_d is not None and b == 0:
                        for di, sl in enumerate((4, 2, 5, 1, 0, 3)):
                            g.dma_start(out=dbg_d[t, di], in_=S[sl][:, 0:TT]).then_inc(s_poll, 16)
                        if t > 0:
                            g.dma_start(out=dbg_d[t, 6][:, 0:NB], in_=ysave[:]).then_inc(s_poll, 16)
                            g.dma_start(out=dbg_d[t, 7][:, 0:NB], in_=csave[:]).then_inc(s_poll, 16)
                        g.wait_ge(s_poll, 128 * (t + 1) - 32)
                    g.tensor_tensor(out=x1b[:, b * TT:(b + 1) * TT],
                                    in0=x1b[:, b * TT:(b + 1) * TT],
                                    in1=S[4][:, 0:TT], op=ALU.add).then_inc(s_gx, 1)
                    g.dma_start(out=x2p_spill[t, b],
                                in_=x1b[:, b * TT:(b + 1) * TT]).then_inc(s_spill, 16)
                    g.wait_ge(s_sp9, NB * t + b + 1)
                    g.dma_start(out=p_d.ap()[t, b],
                                in_=pstage[:, (b % 2) * TT:(b % 2 + 1) * TT]).then_inc(s_pspill, 16)
                    if b == 14 and t + 1 < NT:
                        g.wait_ge(s_tgate, 32 * (t + 1))
                        g.dma_start(
                            out=xbf[:].rearrange("p (k t) -> p k t", k=NK),
                            in_=x_in[:, (t + 1) * TT:(t + 2) * TT].rearrange(
                                "(k p) t -> p k t", p=128),
                        ).then_inc(s_xbf, 16)
            if mode != "full":
                if dbgf_d is not None:
                    g.wait_ge(s_out, 16 * NB * NT)
                    g.dma_start(out=dbgf_d[12:13, 0:128],
                                in_=asumn[0:1, :]).then_inc(s_poll, 16)
                    g.dma_start(out=dbgf_d[12:13, 128:136],
                                in_=rsumn[0:1, :]).then_inc(s_poll, 16)
                    g.dma_start(out=dbgf_d[13:14, 0:1024],
                                in_=wsumn[0:1, :].bitcast(F32)).then_inc(s_poll, 16)
                    g.dma_start(out=dbgf_d[13:14, 1024:1088],
                                in_=asumn[0:1, :].bitcast(F32)).then_inc(s_poll, 16)
                    g.dma_start(out=dbgf_d[13:14, 1088:1092],
                                in_=rsumn[0:1, :].bitcast(F32)).then_inc(s_poll, 16)
                    g.dma_start(out=dbgf_d[14:15, 0:1024],
                                in_=mbf[0:1, :]).then_inc(s_poll, 16)
                    g.dma_start(out=dbgf_d[15:16, 0:512],
                                in_=mbf[0:1, :].bitcast(F32)).then_inc(s_poll, 16)
                    g.dma_start(out=dbgf_d[17:18, 0:1024],
                                in_=topguard[0:1, :]).then_inc(s_poll, 16)
                    g.dma_start(out=dbgf_d[18:19, 0:1024],
                                in_=topguard[127:128, :]).then_inc(s_poll, 16)
                    g.wait_ge(s_poll, 16 * (7 + 2 * NT + 9))
                    g.wait_ge(s_ydbg, 32)
                return
            g.wait_ge(s_ysv, 2 * NB * NT)
            with (
                g.register("slotr") as slotr,
                g.register("nonr") as nonr,
                g.register("more") as more,
                g.register("cnt") as cnt,
                g.register("w") as w,
            ):
                g.wait_ge(s_ld, 64)
                g.reg_load(slotr, sbslot[0:1, 0:1])
                g.reg_load(nonr, sbnon[0:1, 0:1])
                sidx = g.snap(slotr, min_val=0, max_val=1)
                g.dma_start(out=ex_d.ap()[sidx], in_=ysave[:]).then_inc(s_exch, 16)
                g.wait_ge(s_exch, 16)
                g.dma_start(out=fl_d.ap()[sidx], in_=sbnon[:]).then_inc(s_exch, 16)
                g.wait_ge(s_exch, 32)
                g.reg_mov(more, 1)
                with g.While(more):
                    g.dma_start(out=sbfl[:],
                                in_=fl_d.ap().rearrange("a b c -> b (a c)")).then_inc(s_poll, 16)
                    g.drain()
                    g.reg_mov(cnt, 0)
                    for sl in range(2):
                        for k in range(4):
                            g.reg_load(w, sbfl[0:1, 8 * sl + k:8 * sl + k + 1])
                            with g.If_eq(w, nonr):
                                g.reg_add(cnt, cnt, 1)
                    g.reg_mov(more, 1)
                    with g.If_eq(cnt, 8):
                        g.reg_mov(more, 0)
                g.dma_start(out=hraw[:], in_=ex_d.ap()[0]).then_inc(s_exch, 16)
            g.wait_ge(s_exch, 48)
            g.tensor_scalar(out=hfin[:], in0=hraw[:], scalar1=hmk[:, 0:1],
                            scalar2=None, op0=ALU.mult).then_inc(s_hfin, 1)
            g.wait_ge(s_pspill, 256 * NT)
            for t in range(NT):
                for b in range(NB):
                    if NB * t + b >= 2:
                        g.wait_ge(s_gx, NB * NT + 2 * (NB * t + b - 1))
                    g.dma_start(out=pstage[:, (b % 2) * TT:(b % 2 + 1) * TT],
                                in_=p_d.ap()[t, b]).then_inc(s_pldP[b % 2], 16)

        @block.tensor
        def _(t_):
            def setup_sums():
                t_.wait_ge(s_cwb, 256)
                t_.wait_ge(s_cmisc, 48)
                for k in range(NK):
                    t_.matmul(pool[2][0:1, 0:128], lhsT=onesb[:, 0:1],
                              rhs=sbAT[:, k * 128:(k + 1) * 128],
                              start=(k == 0), stop=(k == NK - 1))
                t_.drain().then_inc(s_tsetup, 1)
                for k in range(NK):
                    t_.matmul(pool[3][0:1, 0:8], lhsT=onesb[:, 0:1],
                              rhs=sbRt[:, k * 8:(k + 1) * 8],
                              start=(k == 0), stop=(k == NK - 1))
                t_.drain().then_inc(s_tsetup, 1)
                for b in range(NB):
                    t_.wait_ge(s_wsldP[b % 2], 16 * (b // 2 + 1))
                    if b >= 2:
                        t_.wait_ge(s_vwsum, b - 1)
                    for k in range(NK):
                        t_.matmul(pool[4 + (b % 2)][0:1, 0:128], lhsT=onesb[:, 0:1],
                                  rhs=wpool[:, (b % 2) * D + k * 128:(b % 2) * D + (k + 1) * 128],
                                  start=(k == 0), stop=(k == NK - 1))
                    t_.drain().then_inc(s_wsum, 1)

            for f, dst in ffns:
                t = f if f < NT else f - NT
                inst = inst_of(f)
                rb = 4 * inst
                if f == 0:
                    t_.wait_ge(s_vinit, 8)
                t_.wait_ge(s_vst, 2 * (inst + 1))
                for j in range(4):
                    src = bacc if j < 2 else bacc2
                    h = j % 2
                    t_.wait_ge(s_vrow, rb + j)
                    t_.matmul(ps_row[0:1, 0:512], lhsT=onesb[:, 0:1],
                              rhs=src[:, h * 512:(h + 1) * 512],
                              start=True, stop=True)
                    t_.drain().then_inc(s_trow, 1)
                t_.wait_ge(s_vrsh, f + 1)
                if f > 0:
                    t_.wait_ge(s_vevac, 32 * f)
                    if gate_tiles and f <= NT:
                        t_.wait_ge(s_sevac, 32 * min(f, NT))
                for h in range(2):
                    mm = t_.matmul(pool[h][:, 0:512], lhsT=onesf[0:1, :],
                                   rhs=rows[0:1, 2 * TT + h * 512:2 * TT + (h + 1) * 512],
                                   start=True, stop=True)
                t_.drain().then_inc(s_tbc1, 1)
                if f == 0:
                    setup_sums()
                    t_.wait_ge(s_vsetup, 2)
                    t_.wait_ge(s_vwsum, NB)
                t_.wait_ge(s_smbf, inst + 1)
                for c8 in range(8):
                    for k in range(NK):
                        t_.matmul(pool[4][:, 8 * c8:8 * c8 + 8],
                                  lhsT=xbf[:, k * TT + c8 * 128:k * TT + (c8 + 1) * 128],
                                  rhs=sbRt[:, k * 8:(k + 1) * 8],
                                  start=(k == 0), stop=False)
                    mm = t_.matmul(pool[4][:, 8 * c8:8 * c8 + 8],
                                   lhsT=mbf[0:1, c8 * 128:(c8 + 1) * 128],
                                   rhs=rsumn[0:1, :], start=False, stop=True)
                t_.drain().then_inc(s_trt, 1)
                for h in range(2):
                    for k in range(NK):
                        t_.matmul(pool[2 + h][:, 0:512],
                                  lhsT=sbAT[:, k * 128:(k + 1) * 128],
                                  rhs=xbf[:, k * TT + h * 512:k * TT + (h + 1) * 512],
                                  start=(k == 0), stop=False)
                    t_.matmul(pool[2 + h][:, 0:512], lhsT=asumn[0:1, :],
                              rhs=mbf[0:1, h * 512:(h + 1) * 512],
                              start=False, stop=True)
                    t_.drain().then_inc(s_tA, 1)
                t_.wait_ge(s_voh, f + 1)
                for c8 in range(8):
                    mm = t_.transpose(ps_tp[0:8, c8 * 128:(c8 + 1) * 128],
                                      in_=oh_sb[:, 8 * c8:8 * (c8 + 1)], identity=idbf[:])
                t_.drain().then_inc(s_ttp, 1)
                t_.wait_ge(s_soh, f + 1)
                for h in range(2):
                    mm = t_.matmul(pool[4 + h][:, 0:512], lhsT=exbf[:, :],
                                   rhs=ohT[0:8, h * 512:(h + 1) * 512],
                                   start=True, stop=True)
                t_.drain().then_inc(s_tmask, 1)
                for u in range(32):
                    b, h = u // 2, u % 2
                    bank = u % 6
                    if u >= 6:
                        t_.wait_ge(s_vevac, 32 * f + u - 5)
                    elif bank in (0, 1):
                        t_.wait_ge(s_sbc1, 2 * (f + 1))
                    elif bank in (2, 3):
                        t_.wait_ge(s_vhbf, 2 * (f + 1))
                    else:
                        t_.wait_ge(s_smask, 2 * (f + 1))
                    t_.wait_ge(s_wbP[(NK * f + b) % 2], 16 * ((NK * f + b) // 2 + 1))
                    wslot = (NK * f + b) % 2
                    for k in range(NK):
                        t_.matmul(pool[bank][:, 0:512],
                                  lhsT=wpool[:, wslot * D + k * 128:wslot * D + (k + 1) * 128],
                                  rhs=xbf[:, k * TT + h * 512:k * TT + (h + 1) * 512],
                                  start=(k == 0), stop=False)
                    t_.matmul(pool[bank][:, 0:512],
                              lhsT=wsumn[0:1, b * 128:(b + 1) * 128],
                              rhs=mbf[0:1, h * 512:(h + 1) * 512],
                              start=False, stop=False)
                    t_.matmul(pool[bank][:, 0:512],
                              lhsT=sbBwT[:, b * 128:(b + 1) * 128],
                              rhs=hbf[:, h * 512:(h + 1) * 512],
                              start=False, stop=True)
                    t_.drain().then_inc(s_tbase, 1)
                if f < NT and t in gate_tiles:
                    inst2 = 2 * t + 1
                    rb2 = 4 * inst2
                    t_.wait_ge(s_vst, 2 * (inst2 + 1))
                    for j in range(4):
                        src = bacc if j < 2 else bacc2
                        h = j % 2
                        t_.wait_ge(s_vrow, rb2 + j)
                        t_.matmul(ps_row[0:1, 0:512], lhsT=onesb[:, 0:1],
                                  rhs=src[:, h * 512:(h + 1) * 512],
                                  start=True, stop=True)
                        t_.drain().then_inc(s_trow, 1)
                    t_.wait_ge(s_sfin, inst2 + 1)
                    t_.wait_ge(s_vfin, inst2 + 1)
                    t_.wait_ge(s_vevac, 32 * (f + 1))
                    for j in range(4):
                        srcoff = (0 if j < 2 else TT) + (j % 2) * 512
                        mm = t_.matmul(pool[j][:, 0:512], lhsT=onesf[0:1, :],
                                       rhs=rows[0:1, srcoff:srcoff + 512],
                                       start=True, stop=True)
                    t_.drain().then_inc(s_tbc2, 1)
                    for c in range(32):
                        b, gate = c // 2, c % 2
                        cg = 32 * t + c
                        bank0, bank1 = (2 * c) % 6, (2 * c + 1) % 6
                        if c >= 3:
                            t_.wait_ge(s_sevac, cg - 2)
                        else:
                            t_.wait_ge(s_sbc2, 4 * (t + 1))
                        _ig = NK * t + b
                        t_.wait_ge((s_wgaP if gate == 0 else s_wgxP)[_ig % 2],
                                   16 * (_ig // 2 + 1))
                        woff = ((2 if gate == 0 else 4) + b % 2) * D
                        for h in range(2):
                            t_.wait_ge(s_vap, 32 * t + 16 * (h + 1))
                            for k in range(NK):
                                mm = t_.matmul(pool[(bank0, bank1)[h]][:, 0:512],
                                               lhsT=wpool[:, woff + k * 128:woff + (k + 1) * 128],
                                               rhs=xbf[:, k * TT + h * 512:k * TT + (h + 1) * 512],
                                               start=(k == 0), stop=(k == NK - 1))
                        t_.drain().then_inc(s_tgate, 1)

        @block.scalar
        def _(s):
            s.wait_ge(s_vinit, 8)
            s.activation(out=ksc[:, 0:NB], in_=bload[:, 3 * NB:4 * NB], func=AF.Exp,
                         bias=0.0, scale=-1.0).then_inc(s_sinit, 1)
            s.wait_ge(s_sinit, 1)
            s.activation(out=ksc[:, NB:2 * NB], in_=ksc[:, 0:NB], func=AF.Ln,
                         bias=1.0, scale=1.0).then_inc(s_sinit, 1)

            for f, dst in ffns:
                t = f if f < NT else f - NT
                inst = inst_of(f)
                if f >= NT and mode == "full":
                    if f == NT:
                        s.wait_ge(s_spill, 256 * NT)
                    for k in range(NK):
                        idx = NB * t + k
                        if t > 0:
                            s.wait_ge(s_vevac, 32 * (NT + t - 1) + 2 * (k + 1))
                        if idx >= 2:
                            s.wait_ge(s_gx, NB * NT + 2 * (idx - 1))
                        s.dma_start(out=x1b[:, k * TT:(k + 1) * TT],
                                    in_=x2p_d.ap()[t, k]).then_inc(s_x2rP[k % 2], 16)
                s.wait_ge(s_vrow, 4 * inst + 2)
                s.copy(out=mbf[0:1, :], in_=rows[0:1, 0:TT]).then_inc(s_smbf, 1)
                s.wait_ge(s_vfin, inst + 1)
                s.activation(out=rows[0:1, 2 * TT:3 * TT], in_=rows[0:1, TT:2 * TT],
                             func=AF.Ln, bias=epsrow[0:1, 0:1], scale=1.0)
                s.activation(out=rows[0:1, TT:2 * TT], in_=rows[0:1, 2 * TT:3 * TT],
                             func=AF.Exp, bias=0.0, scale=-0.5).then_inc(s_sfin, 1)
                if f == 0:
                    s.wait_ge(s_tsetup, 1)
                    s.copy(out=asumn[:], in_=pool[2][0:1, 0:128]).then_inc(s_vsetup, 1)
                    s.wait_ge(s_tsetup, 2)
                    s.copy(out=rsumn[:], in_=pool[3][0:1, 0:8]).then_inc(s_vsetup, 1)
                    for b in range(NB):
                        s.wait_ge(s_wsum, b + 1)
                        s.copy(out=wsumn[0:1, b * 128:(b + 1) * 128],
                               in_=pool[4 + (b % 2)][0:1, 0:128]).then_inc(s_vwsum, 1)
                for c8 in range(8):
                    s.wait_ge(s_vrx, 8 * f + c8 + 1)
                    s.activation(out=tmp_e[:, 8 * c8:8 * c8 + 8],
                                 in_=pool[4][:, 8 * c8:8 * c8 + 8],
                                 func=AF.Exp, bias=rmaxc[:, c8:c8 + 1],
                                 scale=1.0).then_inc(s_srx, 1)
                s.wait_ge(s_tbc1, f + 1)
                s.copy(out=rsBh[:, 0:512], in_=pool[0][:, 0:512]).then_inc(s_sbc1, 1)
                s.copy(out=rsBh[:, 512:TT], in_=pool[1][:, 0:512]).then_inc(s_sbc1, 1)
                s.wait_ge(s_ttp, f + 1)
                s.copy(out=ohT[:], in_=ps_tp[:]).then_inc(s_soh, 1)
                s.wait_ge(s_tmask, f + 1)
                s.copy(out=maskbf[:, 0:512], in_=pool[4][:, 0:512]).then_inc(s_smask, 1)
                s.copy(out=maskbf[:, 512:TT], in_=pool[5][:, 0:512]).then_inc(s_smask, 1)
                if f < NT and t in gate_tiles:
                    inst2 = 2 * t + 1
                    s.sem_inc(s_smbf, 1)
                    s.wait_ge(s_vfin, inst2 + 1)
                    s.activation(out=rows[0:1, 2 * TT:3 * TT], in_=rows[0:1, TT:2 * TT],
                                 func=AF.Ln, bias=epsrow[0:1, 0:1], scale=1.0)
                    s.activation(out=rows[0:1, TT:2 * TT], in_=rows[0:1, 2 * TT:3 * TT],
                                 func=AF.Exp, bias=0.0, scale=-0.5).then_inc(s_sfin, 1)
                    s.wait_ge(s_tbc2, t + 1)
                    s.copy(out=mB[:, 0:512], in_=pool[0][:, 0:512]).then_inc(s_sbc2, 1)
                    s.copy(out=mB[:, 512:TT], in_=pool[1][:, 0:512]).then_inc(s_sbc2, 1)
                    s.copy(out=rsB[:, 0:512], in_=pool[2][:, 0:512]).then_inc(s_sbc2, 1)
                    s.copy(out=rsB[:, 512:TT], in_=pool[3][:, 0:512]).then_inc(s_sbc2, 1)
                    for b in range(NB):
                        bb = NB * t + b
                        if bb > 0:
                            s.wait_ge(s_vy, 2 * bb)
                            s.wait_ge(s_gx, bb)
                        s.wait_ge(s_tgate, 32 * t + 2 * b + 1)
                        for h in range(2):
                            aa = s.activation(out=S[0][:, h * 512:(h + 1) * 512],
                                              in_=pool[(4 * b + h) % 6][:, 0:512],
                                              func=AF.Exp, bias=negba[:, b:b + 1], scale=-1.0)
                        aa.then_inc(s_sevac, 1)
                        s.activation(out=S[1][:, 0:TT], in_=S[0][:, 0:TT],
                                     func=AF.Ln, bias=1.0, scale=1.0)
                        s.activation(out=S[0][:, 0:TT], in_=S[1][:, 0:TT],
                                     func=AF.Exp, bias=0.0, scale=-1.0).then_inc(s_sr, 1)
                        s.wait_ge(s_tgate, 32 * t + 2 * b + 2)
                        for h in range(2):
                            aa = s.activation(out=S[3][:, h * 512:(h + 1) * 512],
                                              in_=pool[(4 * b + 2 + h) % 6][:, 0:512],
                                              func=AF.Exp, bias=negbx[:, b:b + 1], scale=-1.0)
                        aa.then_inc(s_sevac, 1)
                        s.activation(out=S[4][:, 0:TT], in_=S[3][:, 0:TT],
                                     func=AF.Ln, bias=1.0, scale=1.0)
                        s.activation(out=S[3][:, 0:TT], in_=S[4][:, 0:TT],
                                     func=AF.Exp, bias=0.0, scale=-1.0).then_inc(s_si, 1)
                        s.wait_ge(s_vla, bb + 1)
                        s.activation(out=S[2][:, 0:TT], in_=S[1][:, 0:TT],
                                     func=AF.Exp, bias=0.0, scale=1.0).then_inc(s_sa, 1)
                        s.activation(out=S[0][:, 0:TT], in_=S[1][:, 0:TT],
                                     func=AF.Exp, bias=0.0, scale=2.0)
                        s.activation(out=S[5][:, 0:TT], in_=S[0][:, 0:TT],
                                     func=AF.Ln, bias=1.0, scale=-1.0)
                        s.activation(out=S[0][:, 0:TT], in_=S[5][:, 0:TT],
                                     func=AF.Exp, bias=0.0, scale=0.5).then_inc(s_sa, 1)
                        s.wait_ge(s_vy, 2 * (bb + 1))
                        s.copy(out=ysave[:, b:b + 1], in_=S[4][:, TT - 1:TT]).then_inc(s_ysv, 1)
                        s.copy(out=csave[:, b:b + 1], in_=S[3][:, TT - 1:TT]).then_inc(s_ysv, 1)
                        if bb >= 2:
                            s.wait_ge(s_pspill, 16 * (bb - 1))
                        s.activation(out=pstage[:, (b % 2) * TT:(b % 2 + 1) * TT],
                                     in_=S[3][:, 0:TT], func=AF.Exp,
                                     bias=0.0, scale=1.0).then_inc(s_sp9, 1)

        @block.vector
        def _(v):
            v.wait_ge(s_giota, 1)
            v.wait_ge(s_ld, 64)
            if dbgf_d is not None:
                v.memset(topguard[:, 0:1024], 123.25)
            v.memset(onesb[:], -1.0).then_inc(s_vinit, 1)
            v.memset(onesf[:], 1.0).then_inc(s_vinit, 1)
            v.memset(epsrow[:], EPS).then_inc(s_vinit, 1)
            v.tensor_scalar(out=idbf[:], in0=id_i[:], scalar1=0, scalar2=None,
                            op0=ALU.is_equal).then_inc(s_vinit, 1)
            v.tensor_scalar(out=exbf[:], in0=ex_i[:], scalar1=0, scalar2=2.0,
                            op0=ALU.is_equal, op1=ALU.mult).then_inc(s_vinit, 1)
            v.tensor_scalar(out=negba[:], in0=bload[:, NB:2 * NB], scalar1=-1.0,
                            scalar2=None, op0=ALU.mult).then_inc(s_vinit, 1)
            v.tensor_scalar(out=negbx[:], in0=bload[:, 2 * NB:3 * NB], scalar1=-1.0,
                            scalar2=None, op0=ALU.mult).then_inc(s_vinit, 1)
            v.tensor_scalar(out=hbrow[:], in0=bload[:, 0:NB], scalar1=0.5,
                            scalar2=None, op0=ALU.mult).then_inc(s_vinit, 1)
            v.wait_ge(s_sinit, 2)
            v.tensor_scalar(out=k1h[:], in0=ksc[:, NB:2 * NB], scalar1=-8.0,
                            scalar2=None, op0=ALU.mult)

            def stats_partials():
                for k in range(NK):
                    xk = xbf[:, k * TT:(k + 1) * TT]
                    v.tensor_tensor(out=bsq[:, 0:TT], in0=xk, in1=xk, op=ALU.mult)
                    if k == 0:
                        v.tensor_copy(bacc[:, 0:TT], xk)
                        v.tensor_copy(bacc2[:, 0:TT], bsq[:, 0:TT])
                    else:
                        acci = v.tensor_tensor(out=bacc[:, 0:TT], in0=bacc[:, 0:TT],
                                        in1=xk, op=ALU.add)
                        last = v.tensor_tensor(out=bacc2[:, 0:TT], in0=bacc2[:, 0:TT],
                                               in1=bsq[:, 0:TT], op=ALU.add)
                        if k == NK - 1:
                            acci.then_inc(s_vst, 1)
                return last

            def ln_rows_and_finish(inst, is_ffn, f):
                rb = 4 * inst
                for j in range(4):
                    h = j % 2
                    if j == 0 and inst > 0:
                        v.wait_ge(s_smbf, inst)
                    v.wait_ge(s_trow, rb + j + 1)
                    dstoff = (0 if j < 2 else TT) + h * 512
                    v.tensor_scalar(out=rows[0:1, dstoff:dstoff + 512],
                                    in0=ps_row[0:1, 0:512], scalar1=-INV_D,
                                    scalar2=None, op0=ALU.mult).then_inc(s_vrow, 1)
                v.tensor_tensor(out=rows[0:1, 2 * TT:3 * TT], in0=rows[0:1, 0:TT],
                                in1=rows[0:1, 0:TT], op=ALU.mult)
                v.tensor_tensor(out=rows[0:1, TT:2 * TT], in0=rows[0:1, TT:2 * TT],
                                in1=rows[0:1, 2 * TT:3 * TT],
                                op=ALU.subtract).then_inc(s_vfin, 1)
                if is_ffn:
                    v.wait_ge(s_sfin, inst + 1)
                    v.tensor_scalar(out=rows[0:1, 2 * TT:3 * TT],
                                    in0=rows[0:1, TT:2 * TT], scalar1=0.5,
                                    scalar2=None, op0=ALU.mult).then_inc(s_vrsh, 1)

            for f, dst in ffns:
                t = f if f < NT else f - NT
                inst = inst_of(f)
                if f < NT:
                    v.wait_ge(s_xbf, 16 * (t + 1))
                    if inst > 0:
                        v.wait_ge(s_trow, 4 * inst)
                    stats_partials().then_inc(s_vst, 1)
                else:
                    if f == NT:
                        v.wait_ge(s_hfin, 1)
                        v.wait_ge(s_tgate, 32 * NT)
                    v.wait_ge(s_trow, 4 * inst)
                    for k in range(NK):
                        v.wait_ge(s_x2rP[k % 2], 16 * (t * 8 + k // 2 + 1))
                        v.wait_ge(s_pldP[k % 2], 16 * ((NB * t + k) // 2 + 1))
                        v.scalar_tensor_tensor(out=x1b[:, k * TT:(k + 1) * TT],
                                               in0=pstage[:, (k % 2) * TT:(k % 2 + 1) * TT],
                                               scalar=hfin[:, k:k + 1],
                                               in1=x1b[:, k * TT:(k + 1) * TT],
                                               op0=ALU.mult, op1=ALU.add).then_inc(s_gx, 1)
                        v.tensor_scalar(out=xbf[:, k * TT:(k + 1) * TT],
                                        in0=x1b[:, k * TT:(k + 1) * TT],
                                        scalar1=1.0, scalar2=None,
                                        op0=ALU.mult).then_inc(s_gx, 1)
                    stats_partials().then_inc(s_vst, 1)
                ln_rows_and_finish(inst, True, f)
                v.wait_ge(s_trt, f + 1)
                for c8 in range(8):
                    v.tensor_reduce(out=rmaxc[:, c8:c8 + 1], in_=pool[4][:, 8 * c8:8 * c8 + 8],
                                    axis=AX.X, op=ALU.max, negate=True).then_inc(s_vrx, 1)
                v.wait_ge(s_srx, 8 * (f + 1))
                v.tensor_scalar(out=oh_sb[:, 0:64], in0=tmp_e[:, 0:64],
                                scalar1=1.0, scalar2=None,
                                op0=ALU.is_ge).then_inc(s_voh, 1)
                v.wait_ge(s_tA, 2 * f + 2)
                v.wait_ge(s_smask, 2 * (f + 1))
                for h in range(2):
                    v.tensor_tensor(out=hbf[:, h * 512:(h + 1) * 512],
                                    in0=pool[2 + h][:, 0:512],
                                    in1=maskbf[:, h * 512:(h + 1) * 512],
                                    op=ALU.mult).then_inc(s_vhbf, 1)
                v.wait_ge(s_sbc1, 2 * (f + 1))
                if f < NT:
                    v.wait_ge(s_x0, 16 * (f + 1))
                for u in range(32):
                    b, h = u // 2, u % 2
                    bank = u % 6
                    v.wait_ge(s_tbase, 32 * f + u + 1)
                    if dst == "out" and (NK * t + b) >= 2 and h == 0:
                        v.wait_ge(s_out, 16 * (NK * t + b))
                    te = tmp_e[:, 0:512]
                    v.tensor_tensor(out=te, in0=pool[bank][:, 0:512],
                                    in1=rsBh[:, h * 512:(h + 1) * 512], op=ALU.mult)
                    x1sl = x1b[:, b * TT + h * 512:b * TT + (h + 1) * 512]
                    if dst == "x1":
                        v.scalar_tensor_tensor(out=x1sl, in0=te, scalar=hbrow[:, b:b + 1],
                                               in1=x1sl, op0=ALU.add,
                                               op1=ALU.add).then_inc(s_vevac, 1)
                        if t in gate_tiles:
                            v.tensor_tensor(out=bsq[:, h * 512:(h + 1) * 512], in0=x1sl,
                                            in1=x1sl, op=ALU.mult)
                            if b == 0:
                                v.tensor_scalar(out=bacc[:, h * 512:(h + 1) * 512], in0=x1sl,
                                                scalar1=1.0, scalar2=None, op0=ALU.mult)
                                v.tensor_copy(bacc2[:, h * 512:(h + 1) * 512],
                                              bsq[:, h * 512:(h + 1) * 512])
                            else:
                                acci = v.tensor_tensor(out=bacc[:, h * 512:(h + 1) * 512],
                                                in0=bacc[:, h * 512:(h + 1) * 512],
                                                in1=x1sl, op=ALU.add)
                                acc2i = v.tensor_tensor(out=bacc2[:, h * 512:(h + 1) * 512],
                                                in0=bacc2[:, h * 512:(h + 1) * 512],
                                                in1=bsq[:, h * 512:(h + 1) * 512], op=ALU.add)
                                if u == 31:
                                    acci.then_inc(s_vst, 1)
                                    acc2i.then_inc(s_vst, 1)
                    else:
                        stg = S[b % 2][:, h * 512:(h + 1) * 512]
                        v.scalar_tensor_tensor(out=stg, in0=te, scalar=hbrow[:, b:b + 1],
                                               in1=x1sl, op0=ALU.add,
                                               op1=ALU.add).then_inc(s_vevac, 1)
                if f < NT and t in gate_tiles:
                    inst2 = 2 * t + 1
                    ln_rows_and_finish(inst2, False, f)
                    v.wait_ge(s_sbc2, 4 * (t + 1))
                    for h in range(2):
                        for b in range(NB):
                            x1sl = x1b[:, b * TT + h * 512:b * TT + (h + 1) * 512]
                            te = tmp_e[:, 0:512]
                            v.tensor_tensor(out=te, in0=x1sl,
                                            in1=mB[:, h * 512:(h + 1) * 512], op=ALU.subtract)
                            v.tensor_tensor(out=xbf[:, b * TT + h * 512:b * TT + (h + 1) * 512],
                                            in0=te, in1=rsB[:, h * 512:(h + 1) * 512],
                                            op=ALU.mult).then_inc(s_vap, 1)
                    for b in range(NB):
                        bb = NB * t + b
                        v.wait_ge(s_sr, bb + 1)
                        v.tensor_scalar(out=S[1][:, 0:TT], in0=S[0][:, 0:TT],
                                        scalar1=k1h[:, b:b + 1], scalar2=LACLAMP,
                                        op0=ALU.mult, op1=ALU.min).then_inc(s_vla, 1)
                        v.wait_ge(s_sa, 2 * (bb + 1))
                        v.wait_ge(s_si, bb + 1)
                        x1sl = x1b[:, b * TT:(b + 1) * TT]
                        v.tensor_tensor(out=S[4][:, 0:TT], in0=x1sl, in1=mB[:, 0:TT],
                                        op=ALU.subtract)
                        v.tensor_tensor(out=S[5][:, 0:TT], in0=S[4][:, 0:TT],
                                        in1=rsB[:, 0:TT], op=ALU.mult)
                        v.tensor_tensor(out=S[4][:, 0:TT], in0=S[3][:, 0:TT],
                                        in1=S[5][:, 0:TT], op=ALU.mult)
                        v.tensor_tensor(out=S[5][:, 0:TT], in0=S[4][:, 0:TT],
                                        in1=S[0][:, 0:TT], op=ALU.mult)
                        if t > 0:
                            v.wait_ge(s_ysv, 2 * (NB * (t - 1) + b + 1))
                        init_y = 0.0 if t == 0 else ysave[:, b:b + 1]
                        v.tensor_tensor_scan(out=S[4][:, 0:TT], data0=S[2][:, 0:TT],
                                             data1=S[5][:, 0:TT], initial=init_y,
                                             op0=ALU.mult, op1=ALU.add).then_inc(s_vy, 1)
                        init_c = 0.0 if t == 0 else csave[:, b:b + 1]
                        v.tensor_tensor_scan(out=S[3][:, 0:TT], data0=S[1][:, 0:TT],
                                             data1=S[1][:, 0:TT], initial=init_c,
                                             op0=ALU.add, op1=ALU.bypass).then_inc(s_vy, 1)

    return nc


def kernel(L0_attn_x, L0_attn_Wa, L0_attn_Wx, L0_attn_ba, L0_attn_bx,
           L0_attn_Lam, L0_ffn_W_base, L0_ffn_b_base, L0_ffn_router_W,
           L0_ffn_A, L0_ffn_Bw):
    mode = _MODE
    B = 4

    xT = np.ascontiguousarray(np.asarray(L0_attn_x, np.float32).transpose(0, 2, 1))

    def lhsT_blocks(W):
        return np.ascontiguousarray(
            np.asarray(W, np.float32).T.reshape(D, NB, 128).transpose(1, 0, 2))

    wb = lhsT_blocks(L0_ffn_W_base)
    wa = lhsT_blocks(L0_attn_Wa)
    wx = lhsT_blocks(L0_attn_Wx)
    rt = np.ascontiguousarray(np.asarray(L0_ffn_router_W, np.float32).T)
    at = np.ascontiguousarray(np.asarray(L0_ffn_A, np.float32).reshape(128, D).T)
    bwt = np.ascontiguousarray(
        np.asarray(L0_ffn_Bw, np.float32).transpose(0, 2, 1).reshape(128, D))

    def col16(vec):
        return np.ascontiguousarray(np.asarray(vec, np.float32).reshape(NB, 128).T)

    bvec = np.ascontiguousarray(np.concatenate(
        [col16(L0_ffn_b_base), col16(L0_attn_ba),
         col16(L0_attn_bx), col16(L0_attn_Lam)], axis=1))

    nonce = np.full((1, 8), int(time.time_ns() // 1000) % (2**30) + 1, np.int32)

    nc = build_nc(mode)
    in_maps = []
    for c in range(8):
        b, half = c // 2, c % 2
        in_maps.append(dict(
            x=np.ascontiguousarray(xT[b, :, half * TPC:(half + 1) * TPC]),
            wb=wb, wa=wa, wx=wx, rt=rt, at=at, bwt=bwt, bvec=bvec,
            hmask=np.full((128, 1), float(half), np.float32),
            slot=np.array([[half]], np.int32),
            nonce=nonce,
        ))
    res = run_bass_kernel_spmd(nc, in_maps, core_ids=list(range(8)))

    outT = np.empty((B, D, 2 * TPC), np.float32)
    for c in range(8):
        b, half = c // 2, c % 2
        o = res.results[c]["out"]
        o = np.asarray(o).transpose(1, 2, 0, 3).reshape(D, TPC)
        outT[b, :, half * TPC:(half + 1) * TPC] = o
    return np.ascontiguousarray(outT.transpose(0, 2, 1))
